# version 09 (final)
# baseline (speedup 1.0000x reference)
import math
from contextlib import ExitStack
import numpy as np
import ml_dtypes
import concourse.bass as bass
import concourse.mybir as mybir
from concourse.bass_utils import run_bass_kernel_spmd

F32 = mybir.dt.float32
BF16 = mybir.dt.bfloat16
F8 = mybir.dt.float8e4
ALU = mybir.AluOpType
ACTF = mybir.ActivationFunctionType
AXX = mybir.AxisListType.X

B, S, H = 2, 8192, 2048
NK, NV, DK, DV = 16, 32, 128, 128
CH = 64
T = 2048
TP = T + 3
NCH = 32
NPAIR = 16
NT5 = [512, 512, 512, 512, 3]
LNDK = -0.5 * math.log(128.0)


class Ctr:
    def __init__(self, sem, step=1):
        self.sem, self.n, self.step = sem, 0, step

    def inc(self, instr):
        self.n += self.step
        instr.then_inc(self.sem, self.step)
        return instr

    def wait(self, eng, n=None):
        eng.wait_ge(self.sem, self.n if n is None else n)


class DSeq:

    def __init__(self, flush_out, flush_in, nul=None, **flush_kw):
        self.pend = None
        self.flush_out = flush_out
        self.flush_in = flush_in
        self.flush_kw = flush_kw
        self.nul = nul

    def dma(self, ctr, eng, out, in_, **kw):
        d = eng.dma_start(out=out, in_=in_, **kw)
        if self.pend is not None:
            d.then_inc(self.pend[0], self.pend[1])
        elif self.nul is not None:
            d.then_inc(self.nul, 16)
        self.pend = (ctr.sem, ctr.step)
        ctr.n += ctr.step
        return d

    def flush(self, eng):
        if self.pend is not None:
            d = eng.dma_start(out=self.flush_out, in_=self.flush_in,
                              **self.flush_kw)
            d.then_inc(self.pend[0], self.pend[1])
            self.pend = None


def build_l1(debug=False):
    nc = bass.Bass("TRN2", target_bir_lowering=False, detect_race_conditions=False)
    DI = lambda n, s, d=F32: nc.dram_tensor(n, s, d, kind="ExternalInput").ap()
    DO = lambda n, s, d=F32: nc.dram_tensor(n, s, d, kind="ExternalOutput").ap()
    DX = lambda n, s, d=F32: nc.dram_tensor(n, s, d).ap()

    xT = DI("xT", [H, TP])
    wqkvT = DI("wqkvT", [H, 8192])
    wzT = DI("wzT", [H, 4096])
    wbT = DI("wbT", [H, 32])
    waT = DI("waT", [H, 32])
    convw = DI("convw", [8192, 4])
    wmod = DI("wmod", [H, 12288])
    c2d = DI("c2d", [128, 16])
    advec = DI("advec", [32, 2])
    negsel = DI("negsel", [32, 2048])
    selp = DI("selp", [32, 2048])
    maskadd1 = DI("maskadd1", [64, 2048], BF16)
    maskadd2 = DI("maskadd2", [64, 2048], BF16)
    maskadd3 = DI("maskadd3", [64, 2048], BF16)
    onescols = DI("onescols", [128, 32, 32], BF16)
    idpair = DI("idpair", [128, 128], BF16)
    gshI = DI("gsh", [1024, H])
    ushI = DI("ush", [1024, H])
    dshI = DI("dsh", [256, 8192])

    qhTc = DO("qhTc", [NCH, NK, DK, CH], BF16)
    khTc = DO("khTc", [NCH, NK, DK, CH], BF16)
    vf = DO("vf", [4096, T], BF16)
    zf = DO("zf", [4096, T], BF16)
    NTo = DO("NT", [NCH, 128, NPAIR, 128], BF16)
    ATTo = DO("ATT", [NCH, 128, NPAIR, 128], BF16)
    FAC = DO("FAC", [32, 4, T])
    ALPHA = DO("ALPHA", [32, NCH])
    MOD = DO("MOD", [1, 12288])
    DEQG = DO("DEQG", [1024, H], BF16)
    DEQU = DO("DEQU", [1024, H], BF16)
    DEQD = DO("DEQD", [256, 8192], BF16)
    dbg = {}
    if debug:
        dbg["DBGXH"] = DO("DBGXH", [128, 16, TP], BF16)
        dbg["DBGBA"] = DO("DBGBA", [32, 8, T])
        dbg["DBGNSQ"] = DO("DBGNSQ", [32, T])

    qrawD = DX("qraw", [32, 128, T], BF16)
    dsinkD = DX("dsink", [1, 8], BF16)

    with ExitStack() as _ctx:
        ca1c = _ctx.enter_context(nc.sbuf_tensor("ca1c", [128, 16], F32))
        sac = _ctx.enter_context(nc.sbuf_tensor("sac", [128, 16], F32))
        cc = _ctx.enter_context(nc.sbuf_tensor("cc", [128, 16], BF16))
        onesrB = _ctx.enter_context(nc.sbuf_tensor("onesrB", [1, 128], BF16))
        onesrF = _ctx.enter_context(nc.sbuf_tensor("onesrF", [1, 128], F32))
        negonesF = _ctx.enter_context(nc.sbuf_tensor("negonesF", [1, 64], F32))
        onescF = _ctx.enter_context(nc.sbuf_tensor("onescF", [128, 1], F32))
        lndkc = _ctx.enter_context(nc.sbuf_tensor("lndkc", [128, 1], F32))
        onesall = _ctx.enter_context(nc.sbuf_tensor("onesall", [128, 128], F32))
        gcb = _ctx.enter_context(nc.sbuf_tensor("gcb", [32, 2, T], F32))
        adv = _ctx.enter_context(nc.sbuf_tensor("adv", [32, 2], F32))
        negeA = _ctx.enter_context(nc.sbuf_tensor("negeA", [32, 1], F32))
        nsqb = _ctx.enter_context(nc.sbuf_tensor("nsqb", [32, T], F32))
        s_ld = _ctx.enter_context(nc.semaphore("s_ld"))
        s_mm = _ctx.enter_context(nc.semaphore("s_mm"))
        s_ev = _ctx.enter_context(nc.semaphore("s_ev"))
        s_st = _ctx.enter_context(nc.semaphore("s_st"))
        s_ac = _ctx.enter_context(nc.semaphore("s_ac"))
        s_g2 = _ctx.enter_context(nc.semaphore("s_g2"))
        s_sg = _ctx.enter_context(nc.semaphore("s_sg"))
        s_nul = _ctx.enter_context(nc.semaphore("s_nul"))
        spc = _ctx.enter_context(nc.sbuf_tensor("spc", [1, 128], F32))
        alphab = _ctx.enter_context(nc.sbuf_tensor("alphab", [32, NCH], F32))
        dscr = _ctx.enter_context(nc.sbuf_tensor("dscr", [1, 8], F32))

        ld = Ctr(s_ld, 16)
        mm = Ctr(s_mm)
        ev = Ctr(s_ev)
        st = Ctr(s_st, 16)
        ac = Ctr(s_ac)
        g2 = Ctr(s_g2)
        sg = Ctr(s_sg, 16)


        _mod_ctx = ExitStack()
        modrow_t = _mod_ctx.enter_context(nc.sbuf_tensor("modrow_t", [1, 12288], F32))

        with (
            nc.sbuf_tensor("wmf", [128, 4, 512], F32) as wmf,
            nc.sbuf_tensor("wmb", [128, 4, 512], BF16) as wmb,
            nc.sbuf_tensor("ccf", [128, 16], F32) as ccf,
            nc.sbuf_tensor("sgc", [128, 16], F32) as sgc,
            nc.psum_tensor("psmodA", [1, 512], F32) as psmodA,
            nc.psum_tensor("psmodB", [1, 512], F32) as psmodB,
            nc.Block() as blk,
        ):
            psmod = [psmodA, psmodB]
            A_ld, A_ev, A_mm, A_ac = ld.n, ev.n, mm.n, ac.n

            @blk.sync
            def _(sy):
                dq = DSeq(dscr[:], xT[0:1, 0:8], nul=s_nul)
                for t_, s_ in ((ccf[:], c2d), (adv[:], advec)):
                    dq.dma(ld, sy, out=t_, in_=s_)
                for i in range(384):
                    n, k = divmod(i, 16)
                    if i >= 4:
                        ev.wait(sy, A_ev + 1 + (i - 3))
                    dq.dma(ld, sy, 
                        out=wmf[:, i % 4, :],
                        in_=wmod[k * 128:(k + 1) * 128, n * 512:(n + 1) * 512])
                dq.flush(sy)

            @blk.scalar
            def _(a):
                a.wait_ge(s_ld, A_ld + 2 * 16)
                ac.inc(a.activation(sgc[:], ccf[:], ACTF.Sigmoid))
                for n in range(24):
                    a.wait_ge(s_mm, A_mm + (n + 1) * 16)
                    ac.inc(a.copy(modrow_t[0:1, n * 512:(n + 1) * 512], psmod[n % 2][:]))

            @blk.vector
            def _(v):
                v.memset(onesrB[:], 1.0)
                v.memset(onesrF[:], 1.0)
                v.memset(negonesF[:], -1.0)
                v.memset(onescF[:], 1.0)
                v.memset(lndkc[:], LNDK)
                v.memset(onesall[:], 1.0)
                v.wait_ge(s_ac, A_ac + 1)
                v.tensor_mul(cc[:], ccf[:], sgc[:])
                for i in range(384):
                    v.wait_ge(s_ld, A_ld + (2 + i + 1) * 16)
                    if i >= 4:
                        v.wait_ge(s_mm, A_mm + (i - 3))
                    ev.inc(v.tensor_copy(wmb[:, i % 4, :], wmf[:, i % 4, :]))

            @blk.tensor
            def _(t):
                for i in range(384):
                    n, k = divmod(i, 16)
                    t.wait_ge(s_ev, A_ev + (i + 1))
                    if k == 0 and n >= 2:
                        t.wait_ge(s_ac, A_ac + 1 + (n - 1))
                    mm.inc(t.matmul(psmod[n % 2][:], cc[:, k:k + 1], wmb[:, i % 4, :],
                                    start=(k == 0), stop=(k == 15)))

        with nc.Block() as blk:
            B_ld = ld.n

            @blk.sync
            def _(sy):
                dq = DSeq(dscr[:], xT[0:1, 0:8], nul=s_nul)
                st.inc(sy.dma_start(out=MOD, in_=modrow_t[:]))
                for k in range(16):
                    dq.dma(ld, sy, 
                        out=sac[:, k:k + 1],
                        in_=modrow_t[0:1, k * 128:(k + 1) * 128])
                    dq.dma(ld, sy, 
                        out=ca1c[:, k:k + 1],
                        in_=modrow_t[0:1, 2048 + k * 128:2048 + (k + 1) * 128])

                dq.flush(sy)
            @blk.vector
            def _(v):
                v.wait_ge(s_ld, B_ld + 32 * 16)
                ev.inc(v.tensor_scalar(ca1c[:], ca1c[:], 1.0, None, ALU.add))

        _mod_ctx.close()

        with (
            nc.sbuf_tensor("wshC", [128, 2, 4096], F32) as wsh,
            nc.sbuf_tensor("wmaC", [128, 256], F32) as wma,
            nc.sbuf_tensor("wrsC", [128, 256], F32) as wrs,
            nc.sbuf_tensor("wdsC", [128, 256], F32) as wds,
            nc.sbuf_tensor("wq8C", [128, 4096], F8) as wq8,
            nc.sbuf_tensor("wdqC", [128, 2, 4096], BF16) as wdq,
            nc.Block() as blk,
        ):
            tiles = [(gshI[i * 128:(i + 1) * 128, :], DEQG[i * 128:(i + 1) * 128, :], H)
                     for i in range(8)]
            tiles += [(ushI[i * 128:(i + 1) * 128, :], DEQU[i * 128:(i + 1) * 128, :], H)
                      for i in range(8)]
            tiles += [(dshI[i * 128:(i + 1) * 128, j * 4096:(j + 1) * 4096],
                       DEQD[i * 128:(i + 1) * 128, j * 4096:(j + 1) * 4096], 4096)
                      for i in range(2) for j in range(2)]
            C_ld, C_ev, C_st = ld.n, ev.n, sg.n

            @blk.sync
            def _(sy):
                dq = DSeq(dscr[:], xT[0:1, 0:8], nul=s_nul)
                for i, (src, _d, w) in enumerate(tiles):
                    if i >= 2:
                        ev.wait(sy, C_ev + (i - 1))
                    dq.dma(ld, sy, out=wsh[:, i % 2, 0:w], in_=src)

                dq.flush(sy)
            @blk.vector
            def _(v):
                for i, (_s, _d, w) in enumerate(tiles):
                    nb = w // 32
                    v.wait_ge(s_ld, C_ld + (i + 1) * 16)
                    if i >= 2:
                        v.wait_ge(s_sg, C_st + (i - 1) * 16)
                    src3 = wsh[:, i % 2, 0:w].rearrange("p (b q) -> p b q", q=32)
                    v.tensor_reduce(wma[:, 0:nb], src3, AXX, ALU.max,
                                    apply_absolute_value=True)
                    v.memset(spc[:], 0.0)
                    v.tensor_scalar_max(wma[:, 0:nb], wma[:, 0:nb], 4.48e-28)
                    v.memset(spc[:], 0.0)
                    v.reciprocal(wrs[:, 0:nb], wma[:, 0:nb])
                    v.memset(spc[:], 0.0)
                    v.tensor_scalar(wrs[:, 0:nb], wrs[:, 0:nb], 224.0, None, ALU.mult)
                    v.tensor_scalar(wds[:, 0:nb], wma[:, 0:nb], 1.0 / 224.0, None,
                                    ALU.mult)
                    rs_b = bass.AP(wrs, 0, [[256, 128], [1, nb], [0, 32]])
                    ds_b = bass.AP(wds, 0, [[256, 128], [1, nb], [0, 32]])
                    v.tensor_tensor(wq8[:, 0:w].rearrange("p (b q) -> p b q", q=32),
                                    src3, rs_b, ALU.mult)
                    ev.inc(v.tensor_tensor(
                        wdq[:, i % 2, 0:w].rearrange("p (b q) -> p b q", q=32),
                        wq8[:, 0:w].rearrange("p (b q) -> p b q", q=32),
                        ds_b, ALU.mult))

            @blk.gpsimd
            def _(g):
                for i, (_s, dst, w) in enumerate(tiles):
                    g.wait_ge(s_ev, C_ev + i + 1)
                    sg.inc(g.dma_start(out=dst, in_=wdq[:, i % 2, 0:w]))

        _xh_ctx = ExitStack()
        xhatT = _xh_ctx.enter_context(nc.sbuf_tensor("xhatT", [128, 16, TP], BF16))
        _r2_ctx = ExitStack()
        rows2_t = _r2_ctx.enter_context(nc.sbuf_tensor("rows2_t", [128, 5 * TP + 8], F32))
        rrow = rows2_t[0:1, 0:TP]
        mrrow = rows2_t[0:1, TP:2 * TP]
        mrow = rows2_t[0:1, 2 * TP:3 * TP]
        srow = rows2_t[0:1, 3 * TP:4 * TP]
        trow = rows2_t[0:1, 4 * TP:5 * TP]

        with (
            nc.sbuf_tensor("xbufD", [128, 4, 512], F32) as xbuf,
            nc.sbuf_tensor("xsqD", [128, 2, 512], F32) as xsq,
            nc.psum_tensor("psm_m", [1, 512], F32) as psm_m,
            nc.psum_tensor("psm_s", [1, 512], F32) as psm_s,
            nc.Block() as blk,
        ):
            D_ld, D_ev, D_mm, D_ac, D_sg = ld.n, ev.n, mm.n, ac.n, sg.n

            @blk.sync
            def _(sy):
                sy.wait_ge(s_sg, D_sg)
                dq = DSeq(dscr[:], xT[0:1, 0:8], nul=s_nul)
                for i in range(80):
                    ti, k = divmod(i, 16)
                    c0 = sum(NT5[:ti])
                    if i >= 4:
                        mm.wait(sy, D_mm + 2 * (i - 3))
                    dq.dma(ld, sy, out=xbuf[:, i % 4, 0:NT5[ti]],
                                        in_=xT[k * 128:(k + 1) * 128, c0:c0 + NT5[ti]])

                dq.flush(sy)
            @blk.vector
            def _(v):
                for i in range(80):
                    ti = i // 16
                    w = NT5[ti]
                    v.wait_ge(s_ld, D_ld + (i + 1) * 16)
                    if i >= 2:
                        v.wait_ge(s_mm, D_mm + 2 * (i - 1))
                    ev.inc(v.tensor_mul(xsq[:, i % 2, 0:w], xbuf[:, i % 4, 0:w],
                                        xbuf[:, i % 4, 0:w]))

            @blk.tensor
            def _(t):
                for i in range(80):
                    ti, k = divmod(i, 16)
                    w = NT5[ti]
                    t.wait_ge(s_ev, D_ev + i + 1)
                    if k == 0 and ti >= 1:
                        t.wait_ge(s_ac, D_ac + 2 * ti)
                    mm.inc(t.matmul(psm_m[:, 0:w], onescF[:],
                                    xbuf[:, i % 4, 0:w], start=(k == 0),
                                    stop=(k == 15)))
                    mm.inc(t.matmul(psm_s[:, 0:w], onescF[:], xsq[:, i % 2, 0:w],
                                    start=(k == 0), stop=(k == 15)))

            @blk.scalar
            def _(a):
                for ti in range(5):
                    c0 = sum(NT5[:ti])
                    w = NT5[ti]
                    a.wait_ge(s_mm, D_mm + 32 * (ti + 1))
                    ac.inc(a.copy(rows2_t[0:1, 2 * TP + c0:2 * TP + c0 + w], psm_m[:, 0:w]))
                    ac.inc(a.copy(rows2_t[0:1, 3 * TP + c0:3 * TP + c0 + w], psm_s[:, 0:w]))

        with nc.Block() as blk:
            E_ev, E_ac = ev.n, ac.n

            @blk.vector
            def _(v):
                v.tensor_scalar(mrow, mrow, 1.0 / H, None, ALU.mult)
                v.tensor_scalar(srow, srow, 1.0 / H, None, ALU.mult)
                v.tensor_mul(trow, mrow, mrow)
                v.tensor_sub(srow, srow, trow)
                ev.inc(v.tensor_scalar(srow, srow, 1e-5, None, ALU.add))
                v.wait_ge(s_ac, E_ac + 1)
                v.reciprocal(rrow, trow)
                v.tensor_mul(rrow, rrow, srow)
                v.tensor_add(rrow, rrow, trow)
                v.tensor_scalar(rrow, rrow, 0.5, None, ALU.mult)
                v.reciprocal(rrow, rrow)
                v.tensor_mul(mrrow, mrow, rrow)
                ev.inc(v.tensor_scalar(mrrow, mrrow, -1.0, None, ALU.mult))

            @blk.scalar
            def _(a):
                a.wait_ge(s_ev, E_ev + 1)
                ac.inc(a.activation(trow, srow, ACTF.Sqrt))

        with (
            nc.sbuf_tensor("xbufF", [128, 4, 512], F32) as xbuf,
            nc.sbuf_tensor("xt1F", [128, 512], F32) as xt1,
            nc.psum_tensor("psrr", [128, 512], F32) as psrr,
            nc.psum_tensor("psmr", [128, 512], F32) as psmr,
            nc.Block() as blk,
        ):
            F_ld, F_ev, F_mm = ld.n, ev.n, mm.n

            @blk.sync
            def _(sy):
                dq = DSeq(dscr[:], xT[0:1, 0:8], nul=s_nul)
                for i in range(80):
                    ti, k = divmod(i, 16)
                    c0 = sum(NT5[:ti])
                    if i >= 4:
                        ev.wait(sy, F_ev + (i - 3))
                    dq.dma(ld, sy, out=xbuf[:, i % 4, 0:NT5[ti]],
                                        in_=xT[k * 128:(k + 1) * 128, c0:c0 + NT5[ti]])

                dq.flush(sy)
            @blk.tensor
            def _(t):
                for ti in range(5):
                    c0 = sum(NT5[:ti])
                    w = NT5[ti]
                    if ti >= 1:
                        t.wait_ge(s_ev, F_ev + 16 * ti)
                    t.matmul(psrr[:, 0:w], onesrF[:], rows2_t[0:1, c0:c0 + w],
                             start=True, stop=True)
                    mm.inc(t.matmul(psmr[:, 0:w], onesrF[:], rows2_t[0:1, TP + c0:TP + c0 + w],
                                    start=True, stop=True))

            @blk.vector
            def _(v):
                for i in range(80):
                    ti, k = divmod(i, 16)
                    c0 = sum(NT5[:ti])
                    w = NT5[ti]
                    v.wait_ge(s_mm, F_mm + ti + 1)
                    v.wait_ge(s_ld, F_ld + (i + 1) * 16)
                    v.tensor_mul(xt1[:, 0:w], xbuf[:, i % 4, 0:w], psrr[:, 0:w])
                    v.memset(spc[:], 0.0)
                    v.tensor_add(xt1[:, 0:w], xt1[:, 0:w], psmr[:, 0:w])
                    v.memset(spc[:], 0.0)
                    sa_b = bass.AP(sac, k, [[16, 128], [0, w]])
                    ev.inc(v.scalar_tensor_tensor(
                        xhatT[:, k, c0:c0 + w], xt1[:, 0:w], ca1c[:, k:k + 1],
                        sa_b, ALU.mult, ALU.add))

        _r2_ctx.close()


        if debug:
            with nc.Block() as blk:
                @blk.sync
                def _(sy):
                    st.inc(sy.dma_start(out=dbg["DBGXH"], in_=xhatT[:]))

        _bam_ctx = ExitStack()
        bam = _bam_ctx.enter_context(nc.sbuf_tensor("bam", [32, 8, T], F32))
        ysc = _bam_ctx.enter_context(nc.sbuf_tensor("ysc", [32, T], F32))
        sigsc = _bam_ctx.enter_context(nc.sbuf_tensor("sigsc", [32, T], F32))
        b_r = bam[:, 0, :]
        a_r = bam[:, 1, :]
        beta_r = bam[:, 2, :]
        lnb_r = bam[:, 3, :]
        sp_r = bam[:, 4, :]
        g_r = bam[:, 4, :]
        gc_r = bam[:, 4, :]
        gcl_r = bam[:, 5, :]
        s1_r = bam[:, 6, :]
        s2_r = bam[:, 7, :]

        with (
            nc.sbuf_tensor("wbaf", [128, 16, 64], F32) as wbaf,
            nc.sbuf_tensor("wbab", [128, 16, 64], BF16) as wbab,
            nc.psum_tensor("psbG", [32, 512], F32) as psb,
            nc.psum_tensor("psaG", [32, 512], F32) as psa,
            nc.Block() as blk,
        ):
            G_ld, G_ev, G_mm = ld.n, ev.n, mm.n

            @blk.sync
            def _(sy):
                dq = DSeq(dscr[:], xT[0:1, 0:8], nul=s_nul)
                dq.dma(ld, sy, out=wbaf[:, :, 0:32],
                                    in_=wbT.rearrange("(k p) c -> p k c", p=128))
                dq.dma(ld, sy, out=wbaf[:, :, 32:64],
                                    in_=waT.rearrange("(k p) c -> p k c", p=128))

                dq.flush(sy)
            @blk.vector
            def _(v):
                v.wait_ge(s_ld, G_ld + 2 * 16)
                ev.inc(v.tensor_copy(wbab[:], wbaf[:]))
                for ti in range(4):
                    v.wait_ge(s_mm, G_mm + ti + 1)
                    v.tensor_copy(b_r[:, ti * 512:(ti + 1) * 512], psb[:])
                    ev.inc(v.tensor_copy(a_r[:, ti * 512:(ti + 1) * 512], psa[:]))

            @blk.tensor
            def _(t):
                t.wait_ge(s_ev, G_ev + 1)
                for ti in range(4):
                    if ti >= 1:
                        t.wait_ge(s_ev, G_ev + 1 + ti)
                    last = None
                    for k in range(16):
                        t.matmul(psb[:], wbab[:, k, 0:32],
                                 xhatT[:, k, 3 + ti * 512:3 + (ti + 1) * 512],
                                 start=(k == 0), stop=(k == 15))
                        last = t.matmul(psa[:], wbab[:, k, 32:64],
                                        xhatT[:, k, 3 + ti * 512:3 + (ti + 1) * 512],
                                        start=(k == 0), stop=(k == 15))
                    mm.inc(last)

        with nc.Block() as blk:
            @blk.scalar
            def _(a):
                ac.inc(a.activation(beta_r, b_r, ACTF.Sigmoid))
                ac.inc(a.activation(negeA[:], adv[:, 0:1], ACTF.Exp))

            @blk.vector
            def _(v):
                ev.inc(v.tensor_scalar(ysc[:], a_r, adv[:, 1:2], None, ALU.add))

        def emit_softplus(y_src, out_r, sign):
            with nc.Block() as blk:
                @blk.scalar
                def _(a):
                    a.activation(sigsc[:], y_src, ACTF.Sigmoid, scale=-sign)
                    a.activation(s1_r, y_src, ACTF.Relu, scale=-sign)
                    ac.inc(a.activation(s2_r, y_src, ACTF.Relu, scale=sign))
            with nc.Block() as blk:
                @blk.vector
                def _(v):
                    ev.inc(v.tensor_add(s1_r, s1_r, s2_r))
            with nc.Block() as blk:
                @blk.scalar
                def _(a):
                    ac.inc(a.activation(s1_r, s1_r, ACTF.Exp, scale=-1.0))
            with nc.Block() as blk:
                @blk.vector
                def _(v):
                    v.tensor_mul(out_r, s1_r, s1_r)
                    v.tensor_scalar(out_r, out_r, -0.5, None, ALU.mult)
                    v.tensor_add(out_r, out_r, s1_r)
                    ev.inc(v.tensor_add(out_r, out_r, s2_r))
            for _ in range(3):
                with nc.Block() as blk:
                    @blk.scalar
                    def _(a):
                        ac.inc(a.activation(s1_r, out_r, ACTF.Exp))
                with nc.Block() as blk:
                    @blk.vector
                    def _(v):
                        v.tensor_mul(s1_r, s1_r, sigsc[:])
                        v.tensor_scalar(out_r, out_r, 1.0, None, ALU.add)
                        ev.inc(v.tensor_sub(out_r, out_r, s1_r))

        emit_softplus(ysc[:], sp_r, 1.0)
        emit_softplus(b_r, lnb_r, -1.0)

        with nc.Block() as blk:
            @blk.vector
            def _(v):
                v.tensor_scalar(lnb_r, lnb_r, -1.0, None, ALU.mult)
                v.tensor_scalar(g_r, sp_r, negeA[:, 0:1], -1.0, ALU.mult,
                                ALU.mult)
                v.tensor_copy(gc_r, g_r)
                for sh in (1, 2, 4, 8, 16, 32):
                    gv = gc_r.rearrange("p (c i) -> p c i", i=64)
                    sv = s1_r.rearrange("p (c i) -> p c i", i=64)
                    v.memset(spc[:], 0.0)
                    v.tensor_copy(sv[:, :, 0:sh], gv[:, :, 0:sh])
                    v.tensor_add(sv[:, :, sh:64], gv[:, :, sh:64], gv[:, :, 0:64 - sh])
                    v.tensor_copy(gc_r, s1_r)
                ev.inc(v.tensor_add(gcl_r, gc_r, lnb_r))

        with nc.Block() as blk:
            FA_ev, FA_ac = ev.n, ac.n
            gl_b = bass.AP(bam, 4 * T + 63, [[8 * T, 32], [64, 32], [0, 64]])
            gl_s = bass.AP(bam, 4 * T + 63, [[8 * T, 32], [64, 32]])

            @blk.vector
            def _(v):
                gv = gc_r.rearrange("p (c i) -> p c i", i=64)
                sv = s1_r.rearrange("p (c i) -> p c i", i=64)
                v.tensor_sub(sv, gl_b, gv)
                v.tensor_copy(gcb[:, 0, :], gc_r)
                ev.inc(v.tensor_copy(gcb[:, 1, :], gcl_r))

            @blk.scalar
            def _(a):
                a.wait_ge(s_ev, FA_ev + 1)
                if debug:
                    a.wait_ge(s_sg, sg.n + 16)
                ac.inc(a.activation(b_r, gcl_r, ACTF.Exp))
                ac.inc(a.activation(sigsc[:], gc_r, ACTF.Exp, bias=lndkc[0:32, :]))
                ac.inc(a.activation(a_r, s1_r, ACTF.Exp))
                ac.inc(a.activation(alphab[:], gl_s, ACTF.Exp))

            @blk.gpsimd
            def _(g):
                if debug:
                    sg.inc(g.dma_start(out=dbg["DBGBA"], in_=bam[:]))
                g.wait_ge(s_ac, FA_ac + 4)
                sg.inc(g.dma_start(out=FAC[:, 0, :], in_=beta_r))
                sg.inc(g.dma_start(out=FAC[:, 1, :], in_=b_r))
                sg.inc(g.dma_start(out=FAC[:, 2, :], in_=a_r))
                sg.inc(g.dma_start(out=FAC[:, 3, :], in_=sigsc[:]))
                sg.inc(g.dma_start(out=ALPHA, in_=alphab[:]))

        _bam_ctx.close()

        with (
            nc.sbuf_tensor("wqf", [128, 2, 16, 128], F32) as wqf,
            nc.sbuf_tensor("conescols", [128, 32, 32], BF16) as conescols,
            nc.sbuf_tensor("wqb", [128, 2, 16, 128], BF16) as wqb,
            nc.sbuf_tensor("cwc", [128, 4, 4], F32) as cwc,
            nc.sbuf_tensor("mrowH", [128, 1, TP], F32) as mrowH,
            nc.sbuf_tensor("accH", [128, T], F32) as accH,
            nc.sbuf_tensor("sgH", [128, T], F32) as sgH,
            nc.sbuf_tensor("qoutH", [128, T], BF16) as qoutH,
            nc.sbuf_tensor("qsqH", [128, T], BF16) as qsqH,
            nc.psum_tensor("psmxA", [128, 512], F32) as psmxA,
            nc.psum_tensor("psmxB", [128, 512], F32) as psmxB,
            nc.psum_tensor("psnq0", [32, 512], F32) as psnq0,
            nc.psum_tensor("psnq1", [32, 512], F32) as psnq1,
            nc.psum_tensor("psnq2", [32, 512], F32) as psnq2,
            nc.psum_tensor("psnq3", [32, 512], F32) as psnq3,
            nc.Block() as blk,
        ):
            psmx = [psmxA, psmxB]
            psnq = [psnq0, psnq1, psnq2, psnq3]
            H_ld, H_ev, H_mm, H_ac, H_g2, H_st = ld.n, ev.n, mm.n, ac.n, g2.n, sg.n

            def vmark(ct, j):
                return H_ev + 6 * ct + j

            def amark(ct, j):
                return H_ac + 6 * ct + j

            def gmark(ct):
                return H_g2 + ct + 1

            def mcum(ct):
                return H_mm + 9 * min(ct, 32) + 5 * max(0, ct - 32)

            @blk.sync
            def _(sy):
                sy.wait_ge(s_sg, H_st)
                dq = DSeq(dscr[:], xT[0:1, 0:8], nul=s_nul)
                dq.dma(ld, sy, out=conescols[:], in_=onescols)
                for ct in range(64):
                    if ct >= 2:
                        ev.wait(sy, vmark(ct - 2, 1))
                    dq.dma(ld, sy,
                        out=wqf[:, ct % 2],
                        in_=wqkvT[:, ct * 128:(ct + 1) * 128]
                        .rearrange("(k p) c -> p k c", p=128))
                    dq.dma(ld, sy, out=cwc[:, ct % 4, :],
                                        in_=convw[ct * 128:(ct + 1) * 128, :])

                dq.flush(sy)
            @blk.tensor
            def _(t):
                for ct in range(64):
                    t.wait_ge(s_ev, vmark(ct, 1))
                    for tau in range(5):
                        w = NT5[tau]
                        c0 = sum(NT5[:tau])
                        gi = ct * 5 + tau
                        if gi >= 2:
                            pct, ptau = divmod(gi - 2, 5)
                            t.wait_ge(s_ac, amark(pct, ptau + 1))
                        mi = None
                        for k in range(16):
                            mi = t.matmul(psmx[tau % 2][:, 0:w], wqb[:, ct % 2, k, :],
                                          xhatT[:, k, c0:c0 + w],
                                          start=(k == 0), stop=(k == 15))
                        mm.inc(mi)
                    if ct < 32:
                        t.wait_ge(s_ev, vmark(ct, 6))
                        for tau in range(4):
                            mm.inc(t.matmul(psnq[tau][:], conescols[:, ct, :],
                                            qsqH[:, tau * 512:(tau + 1) * 512],
                                            start=(ct == 0), stop=(ct == 31)))

            @blk.scalar
            def _(a):
                for ct in range(64):
                    for tau in range(5):
                        w = NT5[tau]
                        c0 = sum(NT5[:tau])
                        a.wait_ge(s_mm, mcum(ct) + tau + 1)
                        if ct >= 1 and tau == 0:
                            a.wait_ge(s_ev, vmark(ct - 1, 5))
                        ac.inc(a.copy(mrowH[:, 0, c0:c0 + w],
                                      psmx[tau % 2][:, 0:w]))
                    a.wait_ge(s_ev, vmark(ct, 5))
                    ac.inc(a.activation(sgH[:], accH[:], ACTF.Sigmoid))

            @blk.vector
            def _(v):
                for ct in range(64):
                    v.wait_ge(s_ld, H_ld + 16 + (ct + 1) * 2 * 16)
                    if ct >= 2:
                        v.wait_ge(s_mm, mcum(ct - 2) + 5)
                    ev.inc(v.tensor_copy(wqb[:, ct % 2], wqf[:, ct % 2]))
                    v.wait_ge(s_ac, amark(ct, 5))
                    if ct >= 1:
                        v.wait_ge(s_g2, gmark(ct - 1))
                    mr = mrowH[:, 0, :]
                    ev.inc(v.tensor_scalar(accH[:], mr[:, 0:T], cwc[:, ct % 4, 0:1],
                                           None, ALU.mult))
                    ev.inc(v.scalar_tensor_tensor(accH[:], mr[:, 1:1 + T],
                                                  cwc[:, ct % 4, 1:2], accH[:],
                                                  ALU.mult, ALU.add))
                    ev.inc(v.scalar_tensor_tensor(accH[:], mr[:, 2:2 + T],
                                                  cwc[:, ct % 4, 2:3], accH[:],
                                                  ALU.mult, ALU.add))
                    ev.inc(v.scalar_tensor_tensor(accH[:], mr[:, 3:3 + T],
                                                  cwc[:, ct % 4, 3:4], accH[:],
                                                  ALU.mult, ALU.add))
                    v.wait_ge(s_g2, gmark(ct))
                    if ct < 32:
                        ev.inc(v.tensor_mul(qsqH[:], qoutH[:], qoutH[:]))
                    else:
                        ev.inc(v.memset(spc[:], 0.0))

            @blk.gpsimd
            def _(g):
                for ct in range(64):
                    g.wait_ge(s_ac, amark(ct, 6))
                    if ct >= 1:
                        g.wait_ge(s_sg, H_st + ct * 16)
                    g2.inc(g.tensor_mul(qoutH[:], accH[:], sgH[:]))
                    if ct < 32:
                        dst = qrawD[ct]
                    else:
                        dst = vf[(ct - 32) * 128:(ct - 31) * 128, :]
                    sg.inc(g.dma_start(out=dst, in_=qoutH[:]))

        with nc.Block() as blk:
            H2_ac = ac.n

            @blk.scalar
            def _(a):
                for tau, p in enumerate([psnq0, psnq1, psnq2, psnq3]):
                    ac.inc(a.copy(nsqb[:, tau * 512:(tau + 1) * 512], p[:]))

        with nc.Block() as blk:
            @blk.scalar
            def _(a):
                ac.inc(a.activation(nsqb[:], nsqb[:], ACTF.Sqrt))

        with nc.Block() as blk:
            @blk.vector
            def _(v):
                v.tensor_scalar_max(nsqb[:], nsqb[:], 1e-12)
                ev.inc(v.reciprocal(nsqb[:], nsqb[:]))

        if debug:
            with nc.Block() as blk:
                @blk.sync
                def _(sy):
                    st.inc(sy.dma_start(out=dbg["DBGNSQ"], in_=nsqb[:]))

        with (
            nc.sbuf_tensor("wzf", [128, 2, 16, 128], F32) as wzf,
            nc.sbuf_tensor("wzb", [128, 2, 16, 128], BF16) as wzb,
            nc.sbuf_tensor("zob", [128, 2, 512], BF16) as zob,
            nc.psum_tensor("psz0", [128, 512], F32) as psz0,
            nc.psum_tensor("psz1", [128, 512], F32) as psz1,
            nc.Block() as blk,
        ):
            psz = [psz0, psz1]
            I_ld, I_ev, I_mm, I_st = ld.n, ev.n, mm.n, sg.n

            @blk.sync
            def _(sy):
                sy.wait_ge(s_sg, I_st)
                dq = DSeq(dscr[:], xT[0:1, 0:8], nul=s_nul)
                for ct in range(32):
                    if ct >= 2:
                        ev.wait(sy, I_ev + (ct - 2) * 5 + 1)
                    dq.dma(ld, sy,
                        out=wzf[:, ct % 2],
                        in_=wzT[:, ct * 128:(ct + 1) * 128]
                        .rearrange("(k p) c -> p k c", p=128))

                dq.flush(sy)
            @blk.tensor
            def _(t):
                for ct in range(32):
                    t.wait_ge(s_ev, I_ev + ct * 5 + 1)
                    for tau in range(4):
                        i = ct * 4 + tau
                        if i >= 2:
                            pct, ptau = divmod(i - 2, 4)
                            t.wait_ge(s_ev, I_ev + pct * 5 + ptau + 2)
                        mi = None
                        for k in range(16):
                            mi = t.matmul(psz[tau % 2][:], wzb[:, ct % 2, k, :],
                                          xhatT[:, k,
                                                3 + tau * 512:3 + (tau + 1) * 512],
                                          start=(k == 0), stop=(k == 15))
                        mm.inc(mi)

            @blk.vector
            def _(v):
                for ct in range(32):
                    v.wait_ge(s_ld, I_ld + (ct + 1) * 16)
                    if ct >= 2:
                        v.wait_ge(s_mm, I_mm + (ct - 1) * 4)
                    ev.inc(v.tensor_copy(wzb[:, ct % 2], wzf[:, ct % 2]))
                    for tau in range(4):
                        i = ct * 4 + tau
                        v.wait_ge(s_mm, I_mm + i + 1)
                        if i >= 2:
                            v.wait_ge(s_sg, I_st + (i - 1) * 16)
                        ev.inc(v.tensor_copy(zob[:, i % 2, :], psz[tau % 2][:]))

            @blk.gpsimd
            def _(g):
                for ct in range(32):
                    for tau in range(4):
                        i = ct * 4 + tau
                        g.wait_ge(s_ev, I_ev + ct * 5 + tau + 2)
                        sg.inc(g.dma_start(
                            out=zf[ct * 128:(ct + 1) * 128,
                                   tau * 512:(tau + 1) * 512],
                            in_=zob[:, i % 2, :]))


        _xh_ctx.close()
        _rows_ctx = ExitStack()
        rows_t = _rows_ctx.enter_context(nc.sbuf_tensor("rows_t", [128, 16384], F32))

        for tau in range(4):
            with (
                nc.sbuf_tensor("qtl%d" % tau, [128, 4, 512], BF16) as qtl,
                nc.sbuf_tensor("qno%d" % tau, [128, 2, 512], BF16) as qno,
                nc.psum_tensor("psrepA%d" % tau, [128, 512], F32) as psrepA,
                nc.psum_tensor("psrepB%d" % tau, [128, 512], F32) as psrepB,
                nc.Block() as blk,
            ):
                psrep = [psrepA, psrepB]
                H4_ld, H4_ev, H4_mm, H4_st = ld.n, ev.n, mm.n, sg.n

                @blk.sync
                def _(sy):
                    sy.wait_ge(s_sg, H4_st)
                    dq = DSeq(dscr[:], xT[0:1, 0:8], nul=s_nul)
                    srcf = bass.AP(nsqb, tau * 512, [[T, 32], [1, 512]])
                    dst = bass.AP(rows_t, 64 * 16384,
                                  [[16384, 1], [512, 32], [1, 512]])
                    dq.dma(ld, sy, out=dst, in_=srcf)
                    for ct in range(32):
                        if ct >= 4:
                            ev.wait(sy, H4_ev + (ct - 3))
                        dq.dma(ld, sy, 
                            out=qtl[:, ct % 4, :],
                            in_=qrawD[ct, :, tau * 512:(tau + 1) * 512])
                    dq.flush(sy)

                @blk.tensor
                def _(t):
                    t.wait_ge(s_ld, H4_ld + 16)
                    for ct in range(32):
                        t.wait_ge(s_ld, H4_ld + (1 + ct + 1) * 16)
                        if ct >= 2:
                            t.wait_ge(s_ev, H4_ev + (ct - 1))
                        mm.inc(t.matmul(
                            psrep[ct % 2][:], onesall[64:65, :],
                            bass.AP(rows_t, 64 * 16384 + ct * 512,
                                    [[16384, 1], [1, 512]]),
                            start=True, stop=True))

                @blk.vector
                def _(v):
                    for ct in range(32):
                        v.wait_ge(s_mm, H4_mm + ct + 1)
                        if ct >= 2:
                            v.wait_ge(s_sg, H4_st + (ct - 1) * 16)
                        ev.inc(v.tensor_mul(qno[:, ct % 2, :], qtl[:, ct % 4, :],
                                            psrep[ct % 2][:]))

                @blk.gpsimd
                def _(g):
                    for ct in range(32):
                        g.wait_ge(s_ev, H4_ev + ct + 1)
                        kh = ct % 16
                        base = qhTc if ct < 16 else khTc
                        dst = bass.AP(base.tensor, tau * 8 * 131072 + kh * 8192,
                                      [[64, 128], [131072, 8], [1, 64]])
                        sg.inc(g.dma_start(out=dst, in_=qno[:, ct % 2, :]))
                    if tau == 3:
                        sg.inc(g.dma_start(out=dsinkD[:], in_=qno[0:1, 0, 0:8]))

        with ExitStack() as _jctx:
            qcb = _jctx.enter_context(nc.sbuf_tensor("qcb", [128, 2, 16, 64], BF16))
            kcb = _jctx.enter_context(nc.sbuf_tensor("kcb", [128, 2, 16, 64], BF16))
            c_negsel = _jctx.enter_context(nc.sbuf_tensor("c_negsel", [32, 2048], F32))
            c_sel = _jctx.enter_context(nc.sbuf_tensor("c_sel", [32, 2048], F32))
            cmask1 = _jctx.enter_context(nc.sbuf_tensor("cmask1", [64, 2048], BF16))
            cmask2 = _jctx.enter_context(nc.sbuf_tensor("cmask2", [64, 2048], BF16))
            cmask3 = _jctx.enter_context(nc.sbuf_tensor("cmask3", [64, 2048], BF16))
            idp = _jctx.enter_context(nc.sbuf_tensor("idp", [128, 128], BF16))
            rowsJ = _jctx.enter_context(nc.sbuf_tensor("rowsJ", [128, 4096], F32))
            e1b = _jctx.enter_context(nc.sbuf_tensor("e1", [64, 2048], BF16))
            e2b = _jctx.enter_context(nc.sbuf_tensor("e2", [64, 2048], BF16))
            e3b = _jctx.enter_context(nc.sbuf_tensor("e3", [64, 2048], BF16))
            tG = _jctx.enter_context(nc.sbuf_tensor("tG", [64, 2048], F32))
            Bcur = _jctx.enter_context(nc.sbuf_tensor("Bcur", [128, 2048], BF16))
            Bnxt = _jctx.enter_context(nc.sbuf_tensor("Bnxt", [128, 2048], BF16))
            Alev = _jctx.enter_context(nc.sbuf_tensor("Alev", [128, 6, 2048], BF16))
            Xcur = _jctx.enter_context(nc.sbuf_tensor("Xcur", [128, 2048], BF16))
            Xnxt = _jctx.enter_context(nc.sbuf_tensor("Xnxt", [128, 2048], BF16))
            attnP = _jctx.enter_context(nc.sbuf_tensor("attnP", [128, 2048], BF16))
            ps8 = _jctx.enter_context(nc.psum_tensor("ps8", [128, 4096], F32))
            blk = _jctx.enter_context(nc.Block())
            True
            J_ld, J_ev, J_mm, J_ac, J_st = ld.n, ev.n, mm.n, ac.n, sg.n
            TM, VM, AM, GM = 21, 19, 13, 2

            def jm(c, j):
                return J_mm + TM * c + j

            def jv(c, j):
                return J_ev + VM * c + j

            def ja(c, j):
                return J_ac + AM * c + j

            @blk.sync
            def _(sy):
                sy.wait_ge(s_sg, J_st)
                dq = DSeq(dscr[:], xT[0:1, 0:8], nul=s_nul)
                for t_, s_ in ((c_negsel[:], negsel), (c_sel[:], selp),
                               (cmask1[:], maskadd1), (cmask2[:], maskadd2),
                               (cmask3[:], maskadd3), (idp[:], idpair)):
                    dq.dma(ld, sy, out=t_, in_=s_)
                for c in range(NCH):
                    if c >= 1:
                        mm.wait(sy, jm(c - 1, 3))
                    srcf = bass.AP(gcb, c * 64, [[2 * T, 32], [1, 64]])
                    dst = bass.AP(rowsJ, 0, [[4096, 1], [64, 32], [1, 64]])
                    dq.dma(ld, sy, out=dst, in_=srcf)
                    src2 = bass.AP(gcb, T + c * 64, [[2 * T, 32], [1, 64]])
                    dst2 = bass.AP(rowsJ, 2048, [[4096, 1], [64, 32], [1, 64]])
                    dq.dma(ld, sy, out=dst2, in_=src2)
                    if c >= 2:
                        mm.wait(sy, jm(c - 1, 5))
                    dq.dma(ld, sy, 
                        out=qcb[:, c % 2],
                        in_=bass.AP(qhTc.tensor, c * 131072,
                                    [[64, 128], [8192, 16], [1, 64]]))
                    dq.dma(ld, sy,
                        out=kcb[:, c % 2],
                        in_=bass.AP(khTc.tensor, c * 131072,
                                    [[64, 128], [8192, 16], [1, 64]]))
                    dq.dma(ld, sy, out=dscr[:], in_=xT[0:1, 0:8])

                dq.flush(sy)
            @blk.tensor
            def _(t):
                for c in range(NCH):
                    tau, cl = divmod(c, 8)
                    nl = c % 8
                    t.wait_ge(s_ld, J_ld + (6 + 5 * c + 4) * 16)
                    if c >= 1:
                        t.wait_ge(s_ev, jv(c, 0))
                    gcch = gcb[:, 0, c * 64:(c + 1) * 64]
                    gclch = gcb[:, 1, c * 64:(c + 1) * 64]
                    mi = None
                    for n in range(4):
                        t.matmul(ps8[:, n * 512:(n + 1) * 512][0:64, :],
                                 onesrF[:, 0:64],
                                 bass.AP(rowsJ, n * 512, [[4096, 1], [1, 512]]),
                                 start=True, stop=False)
                        mi = t.matmul(ps8[:, n * 512:(n + 1) * 512][0:64, :],
                                      gcch, c_negsel[:, n * 512:(n + 1) * 512],
                                      start=False, stop=True)
                    mm.inc(mi)
                    t.wait_ge(s_ac, ja(c, 1))
                    for n in range(4):
                        t.matmul(ps8[:, n * 512:(n + 1) * 512][0:64, :],
                                 onesrF[:, 0:64],
                                 bass.AP(rowsJ, 2048 + n * 512,
                                         [[4096, 1], [1, 512]]),
                                 start=True, stop=False)
                        mi = t.matmul(ps8[:, n * 512:(n + 1) * 512][0:64, :],
                                      gcch, c_negsel[:, n * 512:(n + 1) * 512],
                                      start=False, stop=True)
                    mm.inc(mi)
                    t.wait_ge(s_ac, ja(c, 2))
                    for n in range(4):
                        t.matmul(ps8[:, n * 512:(n + 1) * 512][0:64, :],
                                 negonesF[:, 0:64],
                                 bass.AP(rowsJ, n * 512, [[4096, 1], [1, 512]]),
                                 start=True, stop=False)
                        mi = t.matmul(ps8[:, n * 512:(n + 1) * 512][0:64, :],
                                      gclch, c_sel[:, n * 512:(n + 1) * 512],
                                      start=False, stop=True)
                    mm.inc(mi)
                    for kh in range(16):
                        mi = t.matmul(ps8[0:64, 2048 + kh * 64:2048 + (kh + 1) * 64],
                                      kcb[:, c % 2, kh, :], kcb[:, c % 2, kh, :],
                                      start=True, stop=True)
                    mm.inc(mi)
                    for kh in range(16):
                        mi = t.matmul(ps8[0:64, 3072 + kh * 64:3072 + (kh + 1) * 64],
                                      kcb[:, c % 2, kh, :], qcb[:, c % 2, kh, :],
                                      start=True, stop=True)
                    mm.inc(mi)
                    t.wait_ge(s_ev, jv(c, 12))
                    for lev in range(5):
                        Bsrc = Bcur if lev % 2 == 0 else Bnxt
                        for m in range(16):
                            mi = t.matmul(ps8[:, m * 128:(m + 1) * 128],
                                          Alev[:, lev, m * 128:(m + 1) * 128],
                                          Bsrc[:, m * 128:(m + 1) * 128],
                                          start=True, stop=True)
                        mm.inc(mi)
                        t.wait_ge(s_ac, ja(c, 3 + 2 * lev + 1))
                        for m in range(16):
                            mi = t.matmul(ps8[:, 2048 + m * 128:2048 + (m + 1) * 128],
                                          Bsrc[:, m * 128:(m + 1) * 128],
                                          Alev[:, lev, m * 128:(m + 1) * 128],
                                          start=True, stop=True)
                        mm.inc(mi)
                        t.wait_ge(s_ac, ja(c, 3 + 2 * lev + 2))
                    t.wait_ge(s_ev, jv(c, 13))
                    for idx, lev in enumerate(range(5, -1, -1)):
                        xs = Xcur if idx % 2 == 0 else Xnxt
                        for m in range(16):
                            mi = t.matmul(ps8[:, m * 128:(m + 1) * 128],
                                          Alev[:, lev, m * 128:(m + 1) * 128],
                                          xs[:, m * 128:(m + 1) * 128],
                                          start=True, stop=True)
                        mm.inc(mi)
                        t.wait_ge(s_ev, jv(c, 14 + idx))

            @blk.vector
            def _(v):
                for c in range(NCH):
                    v.wait_ge(s_mm, jm(c, 1))
                    ev.inc(v.tensor_add(tG[:], ps8[0:64, 0:2048], cmask1[:]))
                    v.wait_ge(s_mm, jm(c, 2))
                    v.wait_ge(s_ac, ja(c, 1))
                    ev.inc(v.tensor_add(tG[:], ps8[0:64, 0:2048], cmask2[:]))
                    v.wait_ge(s_mm, jm(c, 3))
                    v.wait_ge(s_ac, ja(c, 2))
                    ev.inc(v.tensor_add(tG[:], ps8[0:64, 0:2048], cmask3[:]))
                    if c >= 1:
                        v.wait_ge(s_sg, J_st + (2 * c - 1) * 16)
                    ev.inc(v.memset(attnP[:], 0.0))
                    ev.inc(v.memset(Bcur[:], 0.0))
                    ev.inc(v.memset(Alev[:, 0, :], 0.0))
                    v.wait_ge(s_ac, ja(c, 3))
                    v.wait_ge(s_mm, jm(c, 5))
                    pat = ps8[0:64, 3072:4096].rearrange("p (m i) -> p m i", i=64)
                    pkk = ps8[0:64, 2048:3072].rearrange("p (m i) -> p m i", i=64)
                    e1v = e1b[:].rearrange("p (m h i) -> p m h i", h=2, i=64)
                    e2v = e2b[:].rearrange("p (m h i) -> p m h i", h=2, i=64)
                    e3v = e3b[:].rearrange("p (m h i) -> p m h i", h=2, i=64)
                    aPv = attnP[:].rearrange("p (m q) -> p m q", q=128)
                    B0v = Bcur[:].rearrange("p (m q) -> p m q", q=128)
                    A0v = Alev[:, 0, :].rearrange("p (m q) -> p m q", q=128)
                    ev.inc(v.tensor_mul(aPv[0:64, :, 0:64], pat, e1v[:, :, 0, :]))
                    ev.inc(v.tensor_mul(aPv[64:128, :, 64:128], pat,
                                        e1v[:, :, 1, :]))
                    ev.inc(v.scalar_tensor_tensor(B0v[0:64, :, 0:64], pkk, -1.0,
                                                  e2v[:, :, 0, :], ALU.mult, ALU.mult))
                    ev.inc(v.scalar_tensor_tensor(B0v[64:128, :, 64:128], pkk, -1.0,
                                                  e2v[:, :, 1, :], ALU.mult, ALU.mult))
                    ev.inc(v.scalar_tensor_tensor(A0v[0:64, :, 0:64], pkk, -1.0,
                                                  e3v[:, :, 0, :], ALU.mult, ALU.mult))
                    ev.inc(v.scalar_tensor_tensor(A0v[64:128, :, 64:128], pkk, -1.0,
                                                  e3v[:, :, 1, :], ALU.mult, ALU.mult))
                    if c >= 1:
                        v.wait_ge(s_sg, J_st + 2 * c * 16)
                    xi = None
                    for m in range(16):
                        xi = v.tensor_copy(Xcur[:, m * 128:(m + 1) * 128], idp[:])
                    ev.inc(xi)
                    v.memset(spc[:], 0.0)
                    for idx in range(6):
                        v.wait_ge(s_mm, jm(c, 16 + idx))
                        xs = Xcur if idx % 2 == 0 else Xnxt
                        xd = Xnxt if idx % 2 == 0 else Xcur
                        ev.inc(v.tensor_add(xd[:], ps8[:, 0:2048], xs[:]))

            @blk.scalar
            def _(a):
                for c in range(NCH):
                    a.wait_ge(s_ev, jv(c, 1))
                    ac.inc(a.activation(e1b[:], tG[:], ACTF.Exp, bias=lndkc[0:64, :]))
                    a.wait_ge(s_ev, jv(c, 2))
                    ac.inc(a.activation(e2b[:], tG[:], ACTF.Exp))
                    a.wait_ge(s_ev, jv(c, 3))
                    ac.inc(a.activation(e3b[:], tG[:], ACTF.Exp))
                    for lev in range(5):
                        a.wait_ge(s_mm, jm(c, 6 + 2 * lev))
                        ac.inc(a.copy(Bnxt[:] if lev % 2 == 0 else Bcur[:],
                                      ps8[:, 0:2048]))
                        a.wait_ge(s_mm, jm(c, 7 + 2 * lev))
                        ac.inc(a.copy(Alev[:, lev + 1, :], ps8[:, 2048:4096]))

            @blk.gpsimd
            def _(g):
                for c in range(NCH):
                    g.wait_ge(s_ev, jv(c, 12))
                    sg.inc(g.dma_start(
                        out=ATTo[c],
                        in_=attnP[:].rearrange("p (m q) -> p m q", q=128)))
                    g.wait_ge(s_ev, jv(c, VM))
                    sg.inc(g.dma_start(
                        out=NTo[c],
                        in_=Xcur[:].rearrange("p (m q) -> p m q", q=128)))

        _rows_ctx.close()

        with nc.Block() as blk:
            @blk.sync
            def _(sy):
                sy.wait_ge(s_sg, sg.n)
                sy.wait_ge(s_st, st.n)

    return nc, dbg


def kernel(**inputs):
    raise NotImplementedError


NGC = 128


def build_l2(debug=False):
    nc = bass.Bass("TRN2", target_bir_lowering=False, detect_race_conditions=False)
    DI = lambda n, s, d=F32: nc.dram_tensor(n, s, d, kind="ExternalInput").ap()
    DO = lambda n, s, d=F32: nc.dram_tensor(n, s, d, kind="ExternalOutput").ap()

    khTg = DI("khTg", [4, NGC, DK, CH], BF16)
    qhTg = DI("qhTg", [4, NGC, DK, CH], BF16)
    NTg = DI("NTg", [4, NGC, 128, 128], BF16)
    ATTg = DI("ATTg", [4, NGC, 128, 128], BF16)
    ktokg = DI("ktokg", [4, NGC, 128, DK], BF16)
    vtokg = DI("vtokg", [4, NGC, 128, DV], BF16)
    colsB = DI("colsB", [128, 4, NGC])
    colsE = DI("colsE", [128, 4, NGC])
    colsQ = DI("colsQ", [128, 4, NGC])
    colsBt = DI("colsBt", [128, 4, NGC])
    alphaflat = DI("alphaflat", [1, 1024])
    OUTC = DO("OUTC", [4, NGC, 128, DV], BF16)
    if debug:
        DBGW = DO("DBGW", [8, 128, DV], BF16)
        DBGVB = DO("DBGVB", [8, 128, DV], BF16)
        DBGVN = DO("DBGVN", [8, 128, DV], BF16)
        DBGKG = DO("DBGKG", [8, 128, DK], BF16)

    NS = 4 * NGC

    with ExitStack() as _ctx:
        cB = _ctx.enter_context(nc.sbuf_tensor("cB", [128, 4, NGC], F32))
        cE = _ctx.enter_context(nc.sbuf_tensor("cE", [128, 4, NGC], F32))
        cQ = _ctx.enter_context(nc.sbuf_tensor("cQ", [128, 4, NGC], F32))
        cBt = _ctx.enter_context(nc.sbuf_tensor("cBt", [128, 4, NGC], F32))
        afl = _ctx.enter_context(nc.sbuf_tensor("afl", [1, 1024], F32))
        aA = _ctx.enter_context(nc.sbuf_tensor("aA", [128, 512], F32))
        aB = _ctx.enter_context(nc.sbuf_tensor("aB", [128, 512], F32))
        onesr = _ctx.enter_context(nc.sbuf_tensor("onesr", [1, 128], F32))
        Stb = _ctx.enter_context(nc.sbuf_tensor("Stb", [128, 4, 2, 128], BF16))
        tkh = _ctx.enter_context(nc.sbuf_tensor("tkh", [128, 2, CH], BF16))
        tqh = _ctx.enter_context(nc.sbuf_tensor("tqh", [128, 2, CH], BF16))
        tNT = _ctx.enter_context(nc.sbuf_tensor("tNT", [128, 2, 128], BF16))
        tAT = _ctx.enter_context(nc.sbuf_tensor("tAT", [128, 2, 128], BF16))
        tkt = _ctx.enter_context(nc.sbuf_tensor("tkt", [128, 2, DK], BF16))
        tvt = _ctx.enter_context(nc.sbuf_tensor("tvt", [128, 2, DV], BF16))
        kgsb = _ctx.enter_context(nc.sbuf_tensor("kgsb", [128, 2, DK], BF16))
        vbsb = _ctx.enter_context(nc.sbuf_tensor("vbsb", [128, 2, DV], BF16))
        wsb = _ctx.enter_context(nc.sbuf_tensor("wsb", [128, 2, DV], BF16))
        vnsb = _ctx.enter_context(nc.sbuf_tensor("vnsb", [128, 2, DV], BF16))
        osb = _ctx.enter_context(nc.sbuf_tensor("osb", [128, 2, DV], BF16))
        psW = _ctx.enter_context(nc.psum_tensor("psW", [128, 128], F32))
        psV = _ctx.enter_context(nc.psum_tensor("psV", [128, 128], F32))
        psO = _ctx.enter_context(nc.psum_tensor("psO", [128, 128], F32))
        psQ = _ctx.enter_context(nc.psum_tensor("psQ", [128, 128], F32))
        psSA = _ctx.enter_context(nc.psum_tensor("psSA", [128, 128], F32))
        psSB = _ctx.enter_context(nc.psum_tensor("psSB", [128, 128], F32))
        psA = _ctx.enter_context(nc.psum_tensor("psA", [128, 512], F32))
        dscr2 = _ctx.enter_context(nc.sbuf_tensor("dscr2", [1, 8], F32))
        s_ld = _ctx.enter_context(nc.semaphore("s_ld"))
        s_mm = _ctx.enter_context(nc.semaphore("s_mm"))
        s_ev = _ctx.enter_context(nc.semaphore("s_ev"))
        s_ac = _ctx.enter_context(nc.semaphore("s_ac"))
        s_sg = _ctx.enter_context(nc.semaphore("s_sg"))
        s_g2 = _ctx.enter_context(nc.semaphore("s_g2"))
        s_nul = _ctx.enter_context(nc.semaphore("s_nul"))
        spc = _ctx.enter_context(nc.sbuf_tensor("spc", [1, 128], F32))
        blk = _ctx.enter_context(nc.Block())

        def Tm(s, j):
            return 2 + 8 * s + j

        def Vm(s, j):
            return 3 + 7 * s + j

        def Am(s):
            return s + 1

        def Lm(s):
            return (6 + 6 * (s + 1)) * 16

        def step_mg(s):
            return divmod(s, 4)

        @blk.sync
        def _(sy):
            ld2 = Ctr(s_ld, 16)
            dq = DSeq(dscr2[:], colsB[0:1, 0:1, 0:8], nul=s_nul)
            for t_, s_ in ((cB[:], colsB), (cE[:], colsE), (cQ[:], colsQ),
                           (cBt[:], colsBt), (afl[:], alphaflat)):
                dq.dma(ld2, sy, out=t_, in_=s_)
            dq.dma(ld2, sy, out=cB[0:1, 0:1, 0:1], in_=colsB[0:1, 0:1, 0:1])
            for s in range(NS):
                gc, m = step_mg(s)
                if s >= 2:
                    sy.wait_ge(s_mm, Tm(s - 2, 8))
                    sy.wait_ge(s_ev, Vm(s - 2, 2))
                dq.dma(ld2, sy, out=tkh[:, s % 2, :], in_=khTg[m, gc])
                dq.dma(ld2, sy, out=tqh[:, s % 2, :], in_=qhTg[m, gc])
                dq.dma(ld2, sy, out=tNT[:, s % 2, :], in_=NTg[m, gc])
                dq.dma(ld2, sy, out=tAT[:, s % 2, :], in_=ATTg[m, gc])
                dq.dma(ld2, sy, out=tkt[:, s % 2, :], in_=ktokg[m, gc])
                dq.dma(ld2, sy, out=tvt[:, s % 2, :], in_=vtokg[m, gc])
            dq.flush(sy)

        @blk.vector
        def _(v):
            v.memset(onesr[:], 1.0)
            v.memset(Stb[:], 0.0)
            v.wait_ge(s_ld, 6 * 16)
            v.tensor_scalar(cB[:], cB[:], -1.0, None, ALU.mult).then_inc(s_ev, 1)
            v.wait_ge(s_mm, 1)
            v.tensor_copy(aA[:], psA[:]).then_inc(s_ev, 1)
            v.wait_ge(s_mm, 2)
            v.tensor_copy(aB[:], psA[:]).then_inc(s_ev, 1)
            for s in range(NS):
                gc, m = step_mg(s)
                v.wait_ge(s_ld, Lm(s))
                if s >= 2:
                    v.wait_ge(s_mm, Tm(s - 2, 8))
                v.tensor_scalar(kgsb[:, s % 2, :], tkt[:, s % 2, :],
                                cE[:, m, gc:gc + 1], None,
                                ALU.mult).then_inc(s_ev, 1)
                v.tensor_scalar(vbsb[:, s % 2, :], tvt[:, s % 2, :],
                                cBt[:, m, gc:gc + 1], None,
                                ALU.mult).then_inc(s_ev, 1)
                v.memset(spc[:], 0.0)
                v.wait_ge(s_mm, Tm(s, 2))
                v.scalar_tensor_tensor(wsb[:, s % 2, :], psW[:],
                                       cB[:, m, gc:gc + 1], vbsb[:, s % 2, :],
                                       ALU.mult, ALU.add).then_inc(s_ev, 1)
                v.wait_ge(s_mm, Tm(s, 6))
                v.tensor_scalar(osb[:, s % 2, :], psQ[:], cQ[:, m, gc:gc + 1],
                                None, ALU.mult).then_inc(s_ev, 1)
                v.memset(spc[:], 0.0)
                v.wait_ge(s_mm, Tm(s, 4))
                if s >= 2:
                    v.wait_ge(s_sg, (s - 1) * 16)
                v.tensor_add(osb[:, s % 2, :], osb[:, s % 2, :],
                             psO[:]).then_inc(s_ev, 1)
                v.wait_ge(s_mm, Tm(s, 7))
                v.scalar_tensor_tensor(Stb[:, m, 0, :], Stb[:, m, 0, :],
                                       aA[:, m * 128 + gc:m * 128 + gc + 1],
                                       psSA[:], ALU.mult, ALU.add).then_inc(s_ev, 1)
                v.wait_ge(s_mm, Tm(s, 8))
                v.scalar_tensor_tensor(Stb[:, m, 1, :], Stb[:, m, 1, :],
                                       aB[:, m * 128 + gc:m * 128 + gc + 1],
                                       psSB[:], ALU.mult, ALU.add).then_inc(s_ev, 1)

        @blk.tensor
        def _(t):
            t.wait_ge(s_ld, 6 * 16)
            t.matmul(psA[:], onesr[:], afl[0:1, 0:512], start=True,
                     stop=True).then_inc(s_mm, 1)
            t.wait_ge(s_ev, 2)
            t.matmul(psA[:], onesr[:], afl[0:1, 512:1024], start=True,
                     stop=True).then_inc(s_mm, 1)
            t.wait_ge(s_ev, 3)
            for s in range(NS):
                gc, m = step_mg(s)
                t.wait_ge(s_ld, Lm(s))
                if s >= 4:
                    t.wait_ge(s_ev, Vm(s - 4, 7))
                if s >= 1:
                    t.wait_ge(s_ev, Vm(s - 1, 3))
                t.matmul(psW[0:64, :], tkh[:, s % 2, :], Stb[:, m, 0, :],
                         start=True, stop=True).then_inc(s_mm, 1)
                t.matmul(psW[64:128, :], tkh[:, s % 2, :], Stb[:, m, 1, :],
                         start=True, stop=True).then_inc(s_mm, 1)
                t.wait_ge(s_ev, Vm(s, 3))
                if s >= 1:
                    t.wait_ge(s_ac, Am(s - 1))
                t.matmul(psV[:], tNT[:, s % 2, :], wsb[:, s % 2, :],
                         start=True, stop=True).then_inc(s_mm, 1)
                t.wait_ge(s_ac, Am(s))
                if s >= 1:
                    t.wait_ge(s_ev, Vm(s - 1, 5))
                t.matmul(psO[:], tAT[:, s % 2, :], vnsb[:, s % 2, :],
                         start=True, stop=True).then_inc(s_mm, 1)
                if s >= 1:
                    t.wait_ge(s_ev, Vm(s - 1, 4))
                t.matmul(psQ[0:64, :], tqh[:, s % 2, :], Stb[:, m, 0, :],
                         start=True, stop=True).then_inc(s_mm, 1)
                t.matmul(psQ[64:128, :], tqh[:, s % 2, :], Stb[:, m, 1, :],
                         start=True, stop=True).then_inc(s_mm, 1)
                if s >= 1:
                    t.wait_ge(s_ev, Vm(s - 1, 7))
                t.matmul(psSA[:], kgsb[:, s % 2, :][0:64, :],
                         vnsb[:, s % 2, :][0:64, :],
                         start=True, stop=True).then_inc(s_mm, 1)
                t.matmul(psSB[:], kgsb[:, s % 2, :][64:128, :],
                         vnsb[:, s % 2, :][64:128, :],
                         start=True, stop=True, tile_position=(64, 0)).then_inc(
                    s_mm, 1)

        @blk.scalar
        def _(a):
            for s in range(NS):
                a.wait_ge(s_mm, Tm(s, 3))
                a.copy(vnsb[:, s % 2, :], psV[:]).then_inc(s_ac, 1)

        @blk.gpsimd
        def _(g):
            for s in range(NS):
                gc, m = step_mg(s)
                g.wait_ge(s_ev, Vm(s, 5))
                g.dma_start(out=OUTC[m, gc], in_=osb[:, s % 2, :]).then_inc(
                    s_sg, 16)
                if debug and s < 8:
                    g.wait_ge(s_ev, Vm(s, 7))
                    g.dma_start(out=DBGW[s], in_=wsb[:, s % 2, :]).then_inc(
                        s_g2, 16)
                    g.dma_start(out=DBGVB[s], in_=vbsb[:, s % 2, :]).then_inc(
                        s_g2, 16)
                    g.dma_start(out=DBGVN[s], in_=vnsb[:, s % 2, :]).then_inc(
                        s_g2, 16)
                    g.dma_start(out=DBGKG[s], in_=kgsb[:, s % 2, :]).then_inc(
                        s_g2, 16)
            g.wait_ge(s_sg, NS * 16)
            if debug:
                g.wait_ge(s_g2, 32 * 16)

    return nc


def build_l3(debug=False):
    nc = bass.Bass("TRN2", target_bir_lowering=False, detect_race_conditions=False)
    DI = lambda n, s, d=F32: nc.dram_tensor(n, s, d, kind="ExternalInput").ap()
    DO = lambda n, s, d=F32: nc.dram_tensor(n, s, d, kind="ExternalOutput").ap()
    DX = lambda n, s, d=F32: nc.dram_tensor(n, s, d).ap()

    coreT = DI("coreT", [T, 4096], BF16)
    ztok = DI("ztok", [T, 4096], BF16)
    xtok = DI("xtok", [T, H])
    modv = DI("modv", [1, 12288])
    normw = DI("normw", [1, 4096])
    woutT = DI("woutT", [4096, H])
    deqgT = DI("deqgT", [H, 8192], BF16)
    dequT = DI("dequT", [H, 8192], BF16)
    deqdT = DI("deqdT", [8192, H], BF16)
    OUT = DO("OUT", [T, H])
    dbg = {}
    if debug:
        dbg["DBGX2"] = DO("DBGX2", [T, H])
        dbg["DBGH2"] = DO("DBGH2", [T, H], BF16)
        dbg["DBGNRM"] = DO("DBGNRM", [T, 4096], BF16)
        dbg["DBGMS"] = DO("DBGMS", [16, 128, 256])
        dbg["DBGQ8"] = DO("DBGQ8", [16, 128, H], F8)
        dbg["DBGS4"] = DO("DBGS4", [16, 128, H])

    XD = DO if debug else DX
    x2D = XD("x2D", [T, H])
    dsink3 = DX("dsink3", [1, 8], BF16)
    dsink3f = DX("dsink3f", [1, 8])
    nrmTD = XD("nrmTD", [4096, T], BF16)
    hqTD = XD("hqTD", [H, T], BF16)
    h3qTD = XD("h3qTD", [8192, T], BF16)
    woutB = DX("woutB", [4096, H], BF16)

    with ExitStack() as _ctx:
        onesr = _ctx.enter_context(nc.sbuf_tensor("onesr", [1, 128], F32))
        dscr = _ctx.enter_context(nc.sbuf_tensor("dscr", [1, 8], F32))
        tscr = _ctx.enter_context(nc.sbuf_tensor("tscr", [128, 128], BF16))
        nwrep = _ctx.enter_context(nc.sbuf_tensor("nwrep", [128, 4096], F32))
        garep = _ctx.enter_context(nc.sbuf_tensor("garep", [128, 2048], F32))
        gbrep = _ctx.enter_context(nc.sbuf_tensor("gbrep", [128, 2048], F32))
        cb1rep = _ctx.enter_context(nc.sbuf_tensor("cb1rep", [128, 2048], F32))
        sbrep = _ctx.enter_context(nc.sbuf_tensor("sbrep", [128, 2048], F32))
        s_ld = _ctx.enter_context(nc.semaphore("s_ld"))
        s_mm = _ctx.enter_context(nc.semaphore("s_mm"))
        s_ev = _ctx.enter_context(nc.semaphore("s_ev"))
        s_ac = _ctx.enter_context(nc.semaphore("s_ac"))
        s_g2 = _ctx.enter_context(nc.semaphore("s_g2"))
        s_sg = _ctx.enter_context(nc.semaphore("s_sg"))
        s_tp = _ctx.enter_context(nc.semaphore("s_tp"))
        s_nul = _ctx.enter_context(nc.semaphore("s_nul"))
        spc = _ctx.enter_context(nc.sbuf_tensor("spc", [1, 128], F32))
        ld = Ctr(s_ld, 16)
        mm = Ctr(s_mm)
        ev = Ctr(s_ev)
        ac = Ctr(s_ac)
        g2 = Ctr(s_g2)
        sg = Ctr(s_sg, 16)
        tp = Ctr(s_tp, 16)

        with (
            nc.sbuf_tensor("rowsP", [1, 12288], F32) as rowsP,
            nc.sbuf_tensor("mrowv", [1, 12288], F32) as mrowv,
            nc.sbuf_tensor("rwm", [1, 2048], F32) as rwm,
            nc.sbuf_tensor("rwx", [1, 2048], F32) as rwx,
            nc.sbuf_tensor("rwn", [1, 2048], F32) as rwn,
            nc.psum_tensor("psrA", [128, 512], F32) as psrA,
            nc.psum_tensor("psrB", [128, 512], F32) as psrB,
            nc.sbuf_tensor("wob", [128, 2, 2048], F32) as wob,
            nc.sbuf_tensor("wobb", [128, 2, 2048], BF16) as wobb,
            nc.Block() as blk,
        ):
            psr = [psrA, psrB]
            Q_ld, Q_ev, Q_mm, Q_sg, Q_ac = ld.n, ev.n, mm.n, sg.n, ac.n

            @blk.scalar
            def _(a):
                a.wait_ge(s_ld, Q_ld + 16)
                ac.inc(a.sign(rwm[:], mrowv[0:1, 4096:6144]))
                ac.inc(a.sign(rwn[:], mrowv[0:1, 10240:12288]))

            @blk.sync
            def _(sy):
                dq = DSeq(dscr[:], xtok[0:1, 0:8], nul=s_nul)
                dq.dma(ld, sy, out=mrowv[:], in_=modv)
                dq.dma(ld, sy, out=rowsP[0:1, 8192:12288], in_=normw)
                for i in range(32):
                    if i >= 2:
                        ev.wait(sy, Q_ev + 1 + (i - 1))
                    dq.dma(ld, sy, out=wob[:, i % 2, :],
                                        in_=woutT[i * 128:(i + 1) * 128, :])

                dq.flush(sy)
            @blk.vector
            def _(v):
                v.memset(onesr[:], 1.0)
                v.wait_ge(s_ld, Q_ld + 2 * 16)
                v.wait_ge(s_ac, Q_ac + 2)
                v.tensor_mul(rwx[:], mrowv[0:1, 4096:6144], rwm[:])
                v.tensor_scalar_max(rwx[:], rwx[:], 0.25)
                v.tensor_mul(rowsP[0:1, 0:2048], rwx[:], rwm[:])
                v.tensor_mul(rwx[:], mrowv[0:1, 10240:12288], rwn[:])
                v.tensor_scalar_max(rwx[:], rwx[:], 0.25)
                v.tensor_mul(rowsP[0:1, 2048:4096], rwx[:], rwn[:])
                v.tensor_scalar(rowsP[0:1, 4096:6144], mrowv[0:1, 8192:10240],
                                1.0, None, ALU.add)
                ev.inc(v.tensor_copy(rowsP[0:1, 6144:8192],
                                     mrowv[0:1, 6144:8192]))
                for i in range(32):
                    v.wait_ge(s_ld, Q_ld + (2 + i + 1) * 16)
                    if i >= 2:
                        v.wait_ge(s_sg, Q_sg + (i - 1) * 16)
                    ev.inc(v.tensor_copy(wobb[:, i % 2, :], wob[:, i % 2, :]))
                for j in range(24):
                    v.wait_ge(s_mm, Q_mm + j + 1)
                    if j < 4:
                        dst = garep[:, (j % 4) * 512:(j % 4 + 1) * 512]
                    elif j < 8:
                        dst = gbrep[:, (j % 4) * 512:(j % 4 + 1) * 512]
                    elif j < 12:
                        dst = cb1rep[:, (j % 4) * 512:(j % 4 + 1) * 512]
                    elif j < 16:
                        dst = sbrep[:, (j % 4) * 512:(j % 4 + 1) * 512]
                    else:
                        dst = nwrep[:, (j - 16) * 512:(j - 15) * 512]
                    ev.inc(v.tensor_copy(dst, psr[j % 2][:]))

            @blk.tensor
            def _(t):
                t.wait_ge(s_ev, Q_ev + 1)
                for j in range(24):
                    if j >= 2:
                        t.wait_ge(s_ev, Q_ev + 33 + (j - 1))
                    mm.inc(t.matmul(psr[j % 2][:], onesr[:],
                                    rowsP[0:1, j * 512:(j + 1) * 512],
                                    start=True, stop=True))

            @blk.gpsimd
            def _(g):
                for i in range(32):
                    g.wait_ge(s_ev, Q_ev + 1 + i + 1)
                    sg.inc(g.dma_start(out=woutB[i * 128:(i + 1) * 128, :],
                                       in_=wobb[:, i % 2, :]))

        with (
            nc.sbuf_tensor("ctl", [128, 2, 4096], BF16) as ctl,
            nc.sbuf_tensor("ztl", [128, 2, 4096], BF16) as ztl,
            nc.sbuf_tensor("sq1", [128, 4096], F32) as sq1,
            nc.sbuf_tensor("red1", [128, 32], F32) as red1,
            nc.sbuf_tensor("sig1", [128, 4096], F32) as sig1,
            nc.sbuf_tensor("nrm", [128, 2, 4096], BF16) as nrm,
            nc.sbuf_tensor("ntp", [128, 2, 32, 128], BF16) as ntp,
            nc.Block() as blk,
        ):
            R_ld, R_ev, R_ac, R_tp, R_sg = ld.n, ev.n, ac.n, tp.n, sg.n
            stp1 = 2 if debug else 1

            @blk.sync
            def _(sy):
                dq = DSeq(dscr[:], xtok[0:1, 0:8], nul=s_nul)
                dqt = DSeq(tscr[:], nrm[:, 0, 0:128], nul=s_nul, transpose=True)
                for i in range(18):
                    if i < 16:
                        if i >= 2:
                            ev.wait(sy, R_ev + 7 * (i - 1))
                        dq.dma(ld, sy, out=ctl[:, i % 2, :],
                                            in_=coreT[i * 128:(i + 1) * 128, :])
                        dq.dma(ld, sy, out=ztl[:, i % 2, :],
                                            in_=ztok[i * 128:(i + 1) * 128, :])
                        if i == 15:
                            dq.flush(sy)
                    if i >= 2:
                        j = i - 2
                        ev.wait(sy, R_ev + 7 * j + 7)
                        if j >= 2:
                            sy.wait_ge(s_sg, R_sg + (j - 1) * stp1 * 16)
                        dqt.dma(tp, sy, out=ntp[:, j % 2], in_=nrm[:, j % 2, :],
                                transpose=True)

                dq.flush(sy)
                dqt.flush(sy)
            @blk.scalar
            def _(a):
                for i in range(16):
                    a.wait_ge(s_ld, R_ld + (i + 1) * 2 * 16)
                    if i >= 1:
                        a.wait_ge(s_ev, R_ev + 7 * (i - 1) + 6)
                    ac.inc(a.activation(sig1[:], ztl[:, i % 2, :], ACTF.Sigmoid))
                    a.wait_ge(s_ev, R_ev + 7 * i + 3)
                    ac.inc(a.activation(red1[:], red1[:], ACTF.Sqrt))

            @blk.vector
            def _(v):
                for i in range(16):
                    v.wait_ge(s_ld, R_ld + (i + 1) * 2 * 16)
                    ev.inc(v.tensor_mul(sq1[:], ctl[:, i % 2, :], ctl[:, i % 2, :]))
                    ev.inc(v.tensor_reduce(red1[:],
                                           sq1[:].rearrange("p (h q) -> p h q",
                                                            q=128),
                                           AXX, ALU.add))
                    v.memset(spc[:], 0.0)
                    ev.inc(v.tensor_scalar(red1[:], red1[:], 1.0 / 128, 1e-6,
                                           ALU.mult, ALU.add))
                    v.wait_ge(s_ac, R_ac + 2 * i + 2)
                    v.memset(spc[:], 0.0)
                    ev.inc(v.reciprocal(red1[:], red1[:]))
                    v.memset(spc[:], 0.0)
                    rb = bass.AP(red1, 0, [[32, 128], [1, 32], [0, 128]])
                    if i >= 2:
                        v.wait_ge(s_tp, R_tp + (i - 1) * 16)
                    v.tensor_tensor(sq1[:].rearrange("p (h q) -> p h q", q=128),
                                    ctl[:, i % 2, :]
                                    .rearrange("p (h q) -> p h q", q=128),
                                    rb, ALU.mult)
                    ev.inc(v.tensor_mul(sq1[:], sq1[:], nwrep[:]))
                    ev.inc(v.tensor_mul(sq1[:], sq1[:], sig1[:]))
                    ev.inc(v.tensor_mul(nrm[:, i % 2, :], sq1[:], ztl[:, i % 2, :]))

            @blk.gpsimd
            def _(g):
                for i in range(16):
                    g.wait_ge(s_tp, R_tp + (i + 1) * 16)
                    dst = bass.AP(nrmTD.tensor, i * 128,
                                  [[T, 128], [128 * T, 32], [1, 128]])
                    sg.inc(g.dma_start(out=dst, in_=ntp[:, i % 2]))
                    if debug:
                        sg.inc(g.dma_start(out=dbg["DBGNRM"][i * 128:(i + 1) * 128, :],
                                           in_=nrm[:, i % 2, :]))
                sg.inc(g.dma_start(out=dsink3[:], in_=ntp[0:1, 0, 0, 0:8]))

        with (
            nc.sbuf_tensor("ntl2", [128, 2, 32, 256], BF16) as ntl2,
            nc.sbuf_tensor("wol2", [128, 2, 2048], BF16) as wol2,
            nc.sbuf_tensor("xtl2", [128, 2, 2048], F32) as xtl2,
            nc.sbuf_tensor("x2o", [128, 2, 2048], F32) as x2o,
            nc.sbuf_tensor("scr2", [128, 512], F32) as scr2,
            nc.psum_tensor("psP2", [128, 4096], F32) as psP2,
            nc.Block() as blk,
        ):
            S_ld, S_ev, S_mm, S_sg = ld.n, ev.n, mm.n, sg.n
            nst2 = 4 if debug else 2

            @blk.sync
            def _(sy):
                sy.wait_ge(s_sg, S_sg)
                dq = DSeq(dscr[:], xtok[0:1, 0:8], nul=s_nul)
                for tg in range(8):
                    if tg >= 2:
                        mm.wait(sy, S_mm + (tg - 1) * 32)
                        ev.wait(sy, S_ev + (tg - 1) * 2)
                        sy.wait_ge(s_sg, S_sg + (tg - 1) * nst2 * 16)
                    dq.dma(ld, sy,
                        out=ntl2[:, tg % 2],
                        in_=bass.AP(nrmTD.tensor, tg * 256,
                                    [[T, 128], [128 * T, 32], [1, 256]]))
                    dq.dma(ld, sy, out=xtl2[:, tg % 2, :],
                                        in_=xtok[tg * 256:tg * 256 + 128, :])
                    dq.dma(ld, sy, out=x2o[:, tg % 2, :],
                                        in_=xtok[tg * 256 + 128:tg * 256 + 256, :])
                    for vk in range(32):
                        if tg >= 1 or vk >= 2:
                            mm.wait(sy, S_mm + 32 * tg + vk - 1)
                        dq.dma(ld, sy, out=wol2[:, vk % 2, :],
                                            in_=woutB[vk * 128:(vk + 1) * 128, :])

                dq.flush(sy)
            @blk.tensor
            def _(t):
                for tg in range(8):
                    for vk in range(32):
                        t.wait_ge(s_ld, S_ld + (35 * tg + 3 + vk + 1) * 16)
                        if tg >= 1 and vk == 0:
                            t.wait_ge(s_ev, S_ev + tg * 2)
                        for tt in range(2):
                            for cs in range(4):
                                mi = t.matmul(
                                    psP2[:, tt * 2048 + cs * 512:
                                         tt * 2048 + (cs + 1) * 512],
                                    ntl2[:, tg % 2, vk, tt * 128:(tt + 1) * 128],
                                    wol2[:, vk % 2, cs * 512:(cs + 1) * 512],
                                    start=(vk == 0), stop=(vk == 31))
                        mm.inc(mi)

            @blk.vector
            def _(v):
                for tg in range(8):
                    v.wait_ge(s_mm, S_mm + 32 * (tg + 1))
                    if tg >= 2:
                        v.wait_ge(s_sg, S_sg + (tg - 1) * nst2 * 16)
                    for tt in range(2):
                        xdst = xtl2 if tt == 0 else x2o
                        for cs in range(4):
                            sl = slice(cs * 512, (cs + 1) * 512)
                            pslice = psP2[:, tt * 2048 + cs * 512:
                                          tt * 2048 + (cs + 1) * 512]
                            v.tensor_tensor(scr2[:], pslice, garep[:, sl],
                                            ALU.mult)
                            v.memset(spc[:], 0.0)
                            v.tensor_add(xdst[:, tg % 2, sl],
                                         xdst[:, tg % 2, sl], scr2[:])
                        ev.inc(v.memset(spc[:], 0.0))

            @blk.gpsimd
            def _(g):
                for tg in range(8):
                    g.wait_ge(s_ev, S_ev + tg * 2 + 1)
                    sg.inc(g.dma_start(out=x2D[tg * 256:tg * 256 + 128, :],
                                       in_=xtl2[:, tg % 2, :]))
                    g.wait_ge(s_ev, S_ev + tg * 2 + 2)
                    sg.inc(g.dma_start(out=x2D[tg * 256 + 128:tg * 256 + 256, :],
                                       in_=x2o[:, tg % 2, :]))
                    if debug:
                        sg.inc(g.dma_start(
                            out=dbg["DBGX2"][tg * 256:tg * 256 + 128, :],
                            in_=xtl2[:, tg % 2, :]))
                        sg.inc(g.dma_start(
                            out=dbg["DBGX2"][tg * 256 + 128:tg * 256 + 256, :],
                            in_=x2o[:, tg % 2, :]))
                sg.inc(g.dma_start(out=dsink3f[:], in_=scr2[0:1, 0:8]))

        with (
            nc.sbuf_tensor("x2t4", [128, 2, 2048], F32) as x2t4,
            nc.sbuf_tensor("sq4", [128, 2048], F32) as sq4,
            nc.sbuf_tensor("sc4", [128, 64], F32) as sc4,
            nc.sbuf_tensor("s54", [128, 32], F32) as s54,
            nc.sbuf_tensor("one4", [128, 32], F32) as one4,
            nc.sbuf_tensor("trcs", [128, 256], F32) as trcs,
            nc.sbuf_tensor("ma4", [128, 64], F32) as ma4,
            nc.sbuf_tensor("rs4", [128, 64], F32) as rs4,
            nc.sbuf_tensor("ds4", [128, 64], F32) as ds4,
            nc.sbuf_tensor("q84", [128, 2048], F8) as q84,
            nc.sbuf_tensor("hq4", [128, 2, 2048], BF16) as hq4,
            nc.sbuf_tensor("htp4", [128, 2, 16, 128], BF16) as htp4,
            nc.Block() as blk,
        ):
            U_ld, U_ev, U_ac, U_tp, U_sg = ld.n, ev.n, ac.n, tp.n, sg.n
            stp4 = 4 if debug else 1

            @blk.sync
            def _(sy):
                sy.wait_ge(s_sg, U_sg)
                dq = DSeq(dscr[:], xtok[0:1, 0:8], nul=s_nul)
                dqt = DSeq(tscr[:], hq4[:, 0, 0:128], nul=s_nul, transpose=True)
                for i in range(18):
                    if i < 16:
                        if i >= 2:
                            ev.wait(sy, U_ev + 3 * (i - 1))
                        dq.dma(ld, sy, out=x2t4[:, i % 2, :],
                                            in_=x2D[i * 128:(i + 1) * 128, :])
                        if i == 15:
                            dq.flush(sy)
                    if i >= 2:
                        j = i - 2
                        ev.wait(sy, U_ev + 3 * j + 3)
                        if j >= 2:
                            sy.wait_ge(s_sg, U_sg + (j - 1) * stp4 * 16)
                        dqt.dma(tp, sy, out=htp4[:, j % 2], in_=hq4[:, j % 2, :],
                                transpose=True)

                dq.flush(sy)
                dqt.flush(sy)
            @blk.scalar
            def _(a):
                for i in range(16):
                    a.wait_ge(s_ev, U_ev + 3 * i + 1)
                    ac.inc(a.activation(s54[:], s54[:], ACTF.Sqrt))

            @blk.vector
            def _(v):
                for i in range(16):
                    if i == 0:
                        v.memset(one4[:], 1.0)
                    v.wait_ge(s_ld, U_ld + (i + 1) * 16)
                    if debug and i >= 2:
                        v.wait_ge(s_sg, U_sg + 4 * (i - 1) * 16)
                    xt = x2t4[:, i % 2, :]
                    v.tensor_reduce(sc4[:, 0:1], xt, AXX, ALU.add)
                    v.tensor_mul(sq4[:], xt, xt)
                    v.tensor_reduce(sc4[:, 8:9], sq4[:], AXX, ALU.add)
                    for _ in range(4):
                        v.memset(spc[:], 0.0)
                    v.tensor_scalar(sc4[:, 16:17], sc4[:, 0:1], 1.0 / H, None,
                                    ALU.mult)
                    v.tensor_scalar(sc4[:, 8:9], sc4[:, 8:9], 1.0 / H, None,
                                    ALU.mult)
                    for _ in range(4):
                        v.memset(spc[:], 0.0)
                    v.tensor_scalar(sc4[:, 24:25], sc4[:, 16:17], -1.0, None,
                                    ALU.mult)
                    v.tensor_mul(sc4[:, 32:33], sc4[:, 16:17], sc4[:, 16:17])
                    for _ in range(4):
                        v.memset(spc[:], 0.0)
                    v.tensor_sub(sc4[:, 32:33], sc4[:, 8:9], sc4[:, 32:33])
                    for _ in range(4):
                        v.memset(spc[:], 0.0)
                    v.tensor_scalar(sc4[:, 32:33], sc4[:, 32:33], 1e-5, None,
                                    ALU.add)
                    for _ in range(4):
                        v.memset(spc[:], 0.0)
                    ev.inc(v.tensor_scalar(s54[:], one4[:], sc4[:, 32:33],
                                           None, ALU.mult))
                    v.wait_ge(s_ac, U_ac + i + 1)
                    v.reciprocal(sc4[:, 48:49], s54[:, 0:1])
                    v.memset(spc[:], 0.0)
                    v.memset(spc[:], 0.0)
                    v.memset(spc[:], 0.0)
                    v.memset(spc[:], 0.0)
                    v.memset(spc[:], 0.0)
                    v.tensor_scalar(sq4[:], xt, sc4[:, 24:25], sc4[:, 48:49],
                                    ALU.add, ALU.mult)
                    v.tensor_mul(sq4[:], sq4[:], cb1rep[:])
                    ev.inc(v.tensor_add(sq4[:], sq4[:], sbrep[:]))
                    v.tensor_reduce(ma4[:], sq4[:].rearrange("p (b q) -> p b q",
                                                             q=32),
                                    AXX, ALU.max, apply_absolute_value=True)
                    v.memset(spc[:], 0.0)
                    v.tensor_scalar_max(ma4[:], ma4[:], 4.48e-28)
                    v.memset(spc[:], 0.0)
                    v.reciprocal(rs4[:], ma4[:])
                    v.memset(spc[:], 0.0)
                    v.tensor_scalar(rs4[:], rs4[:], 224.0, None, ALU.mult)
                    v.tensor_scalar(ds4[:], ma4[:], 1.0 / 224.0, None, ALU.mult)
                    v.memset(spc[:], 0.0)
                    v.memset(spc[:], 0.0)
                    rsb = bass.AP(rs4, 0, [[64, 128], [1, 64], [0, 32]])
                    dsb = bass.AP(ds4, 0, [[64, 128], [1, 64], [0, 32]])
                    v.tensor_tensor(q84[:].rearrange("p (b q) -> p b q", q=32),
                                    sq4[:].rearrange("p (b q) -> p b q", q=32),
                                    rsb, ALU.mult)
                    if debug:
                        v.tensor_copy(trcs[:, 0:64], ma4[:])
                        v.tensor_copy(trcs[:, 64:128], rs4[:])
                        v.tensor_copy(trcs[:, 128:192], ds4[:])
                        v.tensor_copy(trcs[:, 192:224], s54[:])
                        v.tensor_copy(trcs[:, 224:256], sc4[:, 0:32])
                    if i >= 2:
                        v.wait_ge(s_tp, U_tp + (i - 1) * 16)
                    ev.inc(v.tensor_tensor(
                        hq4[:, i % 2, :].rearrange("p (b q) -> p b q", q=32),
                        q84[:].rearrange("p (b q) -> p b q", q=32),
                        dsb, ALU.mult))

            @blk.gpsimd
            def _(g):
                for i in range(16):
                    g.wait_ge(s_tp, U_tp + (i + 1) * 16)
                    dst = bass.AP(hqTD.tensor, i * 128,
                                  [[T, 128], [128 * T, 16], [1, 128]])
                    sg.inc(g.dma_start(out=dst, in_=htp4[:, i % 2]))
                    if debug:
                        sg.inc(g.dma_start(
                            out=dbg["DBGH2"][i * 128:(i + 1) * 128, :],
                            in_=hq4[:, i % 2, :]))
                        sg.inc(g.dma_start(out=dbg["DBGMS"][i],
                                           in_=trcs[:]))
                        sg.inc(g.dma_start(out=dbg["DBGQ8"][i], in_=q84[:]))
                sg.inc(g.dma_start(out=dsink3[:], in_=htp4[0:1, 0, 0, 0:8]))

        with (
            nc.sbuf_tensor("wg5", [128, 16, 1024], BF16) as wg5,
            nc.sbuf_tensor("wu5", [128, 16, 1024], BF16) as wu5,
            nc.sbuf_tensor("hqt5", [128, 2, 16, 128], BF16) as hqt5,
            nc.sbuf_tensor("sig5", [128, 1024], F32) as sig5,
            nc.sbuf_tensor("tsl5", [128, 1024], BF16) as tsl5,
            nc.sbuf_tensor("h35", [128, 1024], BF16) as h35,
            nc.sbuf_tensor("ma5", [128, 32], F32) as ma5,
            nc.sbuf_tensor("rs5", [128, 32], F32) as rs5,
            nc.sbuf_tensor("ds5", [128, 32], F32) as ds5,
            nc.sbuf_tensor("q85", [128, 1024], F8) as q85,
            nc.sbuf_tensor("h3q5", [128, 2, 1024], BF16) as h3q5,
            nc.sbuf_tensor("htp5", [128, 2, 8, 128], BF16) as htp5,
            nc.psum_tensor("psP5", [128, 2048], F32) as psP5,
            nc.Block() as blk,
        ):
            V_ld, V_ev, V_mm, V_ac, V_tp, V_sg = ld.n, ev.n, mm.n, ac.n, tp.n, sg.n

            @blk.sync
            def _(sy):
                sy.wait_ge(s_sg, V_sg)
                dq = DSeq(dscr[:], xtok[0:1, 0:8], nul=s_nul)
                dqt = DSeq(tscr[:], h3q5[:, 0, 0:128], nul=s_nul, transpose=True)
                for iq in range(8):
                    if iq >= 1:
                        mm.wait(sy, V_mm + 16 * iq)
                    dq.dma(ld, sy, 
                        out=wg5[:],
                        in_=deqgT[:, iq * 1024:(iq + 1) * 1024]
                        .rearrange("(k p) c -> p k c", p=128))
                    dq.dma(ld, sy, 
                        out=wu5[:],
                        in_=dequT[:, iq * 1024:(iq + 1) * 1024]
                        .rearrange("(k p) c -> p k c", p=128))
                    for tt in range(16):
                        s = iq * 16 + tt
                        if s >= 2:
                            mm.wait(sy, V_mm + s - 1)
                        dq.dma(ld, sy,
                            out=hqt5[:, s % 2],
                            in_=bass.AP(hqTD.tensor, tt * 128,
                                        [[T, 128], [128 * T, 16], [1, 128]]))
                        if tt == 15:
                            dq.dma(ld, sy, out=dscr[:], in_=xtok[0:1, 0:8])
                        if s >= 2:
                            j = s - 2
                            ev.wait(sy, V_ev + 3 * j + 3)
                            if j >= 2:
                                sy.wait_ge(s_sg, V_sg + (j - 1) * 16)
                            dqt.dma(tp, sy, out=htp5[:, j % 2],
                                    in_=h3q5[:, j % 2, :],
                                    transpose=True)
                for j in range(126, 128):
                    ev.wait(sy, V_ev + 3 * j + 3)
                    sy.wait_ge(s_sg, V_sg + (j - 1) * 16)
                    dqt.dma(tp, sy, out=htp5[:, j % 2], in_=h3q5[:, j % 2, :],
                            transpose=True)

                dq.flush(sy)
                dqt.flush(sy)
            @blk.tensor
            def _(t):
                for iq in range(8):
                    for tt in range(16):
                        s = iq * 16 + tt
                        t.wait_ge(s_ld, V_ld + (19 * iq + tt + 3) * 16)
                        if s >= 1:
                            t.wait_ge(s_ac, V_ac + 2 * s)
                            t.wait_ge(s_ev, V_ev + 3 * s - 2)
                        for f in range(16):
                            for cs in range(2):
                                t.matmul(psP5[:, cs * 512:(cs + 1) * 512],
                                         hqt5[:, s % 2, f, :],
                                         wg5[:, f, cs * 512:(cs + 1) * 512],
                                         start=(f == 0), stop=(f == 15))
                                mi = t.matmul(psP5[:, 1024 + cs * 512:
                                                   1024 + (cs + 1) * 512],
                                              hqt5[:, s % 2, f, :],
                                              wu5[:, f, cs * 512:(cs + 1) * 512],
                                              start=(f == 0), stop=(f == 15))
                        mm.inc(mi)

            @blk.scalar
            def _(a):
                for s in range(128):
                    a.wait_ge(s_mm, V_mm + s + 1)
                    for cs in range(2):
                        ac.inc(a.activation(sig5[:, cs * 512:(cs + 1) * 512],
                                            psP5[:, cs * 512:(cs + 1) * 512],
                                            ACTF.Sigmoid))

            @blk.vector
            def _(v):
                for s in range(128):
                    v.wait_ge(s_ac, V_ac + 2 * (s + 1))
                    v.wait_ge(s_mm, V_mm + s + 1)
                    v.tensor_mul(tsl5[:], psP5[:, 0:1024], sig5[:])
                    ev.inc(v.tensor_mul(h35[:], psP5[:, 1024:2048], tsl5[:]))
                    v.tensor_reduce(ma5[:], h35[:].rearrange("p (b q) -> p b q",
                                                             q=32),
                                    AXX, ALU.max, apply_absolute_value=True)
                    v.memset(spc[:], 0.0)
                    v.tensor_scalar_max(ma5[:], ma5[:], 4.48e-28)
                    v.memset(spc[:], 0.0)
                    v.reciprocal(rs5[:], ma5[:])
                    v.memset(spc[:], 0.0)
                    v.tensor_scalar(rs5[:], rs5[:], 224.0, None, ALU.mult)
                    ev.inc(v.tensor_scalar(ds5[:], ma5[:], 1.0 / 224.0, None,
                                           ALU.mult))
                    rsb = bass.AP(rs5, 0, [[32, 128], [1, 32], [0, 32]])
                    dsb = bass.AP(ds5, 0, [[32, 128], [1, 32], [0, 32]])
                    v.tensor_tensor(q85[:].rearrange("p (b q) -> p b q", q=32),
                                    h35[:].rearrange("p (b q) -> p b q", q=32),
                                    rsb, ALU.mult)
                    if s >= 2:
                        v.wait_ge(s_tp, V_tp + (s - 1) * 16)
                    ev.inc(v.tensor_tensor(
                        h3q5[:, s % 2, :].rearrange("p (b q) -> p b q", q=32),
                        q85[:].rearrange("p (b q) -> p b q", q=32),
                        dsb, ALU.mult))

            @blk.gpsimd
            def _(g):
                for iq in range(8):
                    for tt in range(16):
                        s = iq * 16 + tt
                        g.wait_ge(s_tp, V_tp + (s + 1) * 16)
                        dst = bass.AP(h3qTD.tensor, iq * 1024 * T + tt * 128,
                                      [[T, 128], [128 * T, 8], [1, 128]])
                        sg.inc(g.dma_start(out=dst, in_=htp5[:, s % 2]))
                sg.inc(g.dma_start(out=dsink3[:], in_=htp5[0:1, 0, 0, 0:8]))

        with (
            nc.sbuf_tensor("h3t6", [128, 64, 256], BF16) as h3t6,
            nc.sbuf_tensor("wd6", [128, 2, 2048], BF16) as wd6,
            nc.sbuf_tensor("x2t6", [128, 2, 2, 2048], F32) as x2t6,
            nc.sbuf_tensor("scr6", [128, 512], F32) as scr6,
            nc.psum_tensor("psP6", [128, 4096], F32) as psP6,
            nc.Block() as blk,
        ):
            W_ld, W_ev, W_mm, W_sg = ld.n, ev.n, mm.n, sg.n

            @blk.sync
            def _(sy):
                sy.wait_ge(s_sg, W_sg)
                dq = DSeq(dscr[:], xtok[0:1, 0:8], nul=s_nul)
                for tg in range(8):
                    if tg >= 1:
                        mm.wait(sy, W_mm + 64 * tg)
                    if tg >= 2:
                        ev.wait(sy, W_ev + 2 * (tg - 1))
                        sy.wait_ge(s_sg, W_sg + (tg - 1) * 2 * 16)
                    dq.dma(ld, sy,
                        out=h3t6[:],
                        in_=bass.AP(h3qTD.tensor, tg * 256,
                                    [[T, 128], [128 * T, 64], [1, 256]]))
                    dq.dma(ld, sy, out=x2t6[:, tg % 2, 0, :],
                                        in_=x2D[tg * 256:tg * 256 + 128, :])
                    dq.dma(ld, sy, out=x2t6[:, tg % 2, 1, :],
                                        in_=x2D[tg * 256 + 128:tg * 256 + 256, :])
                    for ik in range(64):
                        if tg * 64 + ik >= 2:
                            mm.wait(sy, W_mm + tg * 64 + ik - 1)
                        dq.dma(ld, sy, out=wd6[:, ik % 2, :],
                                            in_=deqdT[ik * 128:(ik + 1) * 128, :])
                    dq.dma(ld, sy, out=dscr[:], in_=xtok[0:1, 0:8])

                dq.flush(sy)
            @blk.tensor
            def _(t):
                for tg in range(8):
                    for ik in range(64):
                        t.wait_ge(s_ld, W_ld + (68 * tg + ik + 4) * 16)
                        if tg >= 1 and ik == 0:
                            t.wait_ge(s_ev, W_ev + 2 * tg)
                        for tt in range(2):
                            for cs in range(4):
                                mi = t.matmul(
                                    psP6[:, tt * 2048 + cs * 512:
                                         tt * 2048 + (cs + 1) * 512],
                                    h3t6[:, ik, tt * 128:(tt + 1) * 128],
                                    wd6[:, ik % 2, cs * 512:(cs + 1) * 512],
                                    start=(ik == 0), stop=(ik == 63))
                        mm.inc(mi)

            @blk.vector
            def _(v):
                for tg in range(8):
                    v.wait_ge(s_mm, W_mm + 64 * (tg + 1))
                    if tg >= 2:
                        v.wait_ge(s_sg, W_sg + (tg - 1) * 2 * 16)
                    for tt in range(2):
                        for cs in range(4):
                            sl = slice(cs * 512, (cs + 1) * 512)
                            pslice = psP6[:, tt * 2048 + cs * 512:
                                          tt * 2048 + (cs + 1) * 512]
                            v.tensor_tensor(scr6[:], pslice, gbrep[:, sl],
                                            ALU.mult)
                            v.memset(spc[:], 0.0)
                            v.tensor_add(x2t6[:, tg % 2, tt, sl],
                                         x2t6[:, tg % 2, tt, sl], scr6[:])
                        ev.inc(v.memset(spc[:], 0.0))

            @blk.gpsimd
            def _(g):
                for tg in range(8):
                    for tt in range(2):
                        g.wait_ge(s_ev, W_ev + tg * 2 + tt + 1)
                        sg.inc(g.dma_start(
                            out=OUT[tg * 256 + tt * 128:tg * 256 + (tt + 1) * 128,
                                    :],
                            in_=x2t6[:, tg % 2, tt, :]))
                g.wait_ge(s_sg, sg.n)

    return nc, dbg


def _l1_inputs(I, cid):
    bb, q = cid // 4, cid % 4
    T0 = q * 2048
    x = I['L0_attn_x'][bb]
    pre = np.zeros((3, 2048), np.float32) if q == 0 else x[T0 - 3:T0]
    xTl = np.ascontiguousarray(np.concatenate([pre, x[T0:T0 + 2048]], 0).T)
    jj, ii = np.meshgrid(np.arange(64), np.arange(64), indexing='ij')
    m1 = np.where(jj <= ii, 0.0, -1e30).astype(np.float32)
    m2 = np.where(jj < ii, 0.0, -1e30).astype(np.float32)
    m3 = np.where(jj > ii, 0.0, -1e30).astype(np.float32)
    sel = np.kron(np.eye(32, dtype=np.float32), np.ones((1, 64), np.float32))
    oc = np.zeros((128, 32, 32), ml_dtypes.bfloat16)
    for j in range(32):
        oc[:, j, j] = 1.0
    d = {
        "xT": xTl,
        "wqkvT": I['L0_attn_W_qkv'].T,
        "wzT": I['L0_attn_W_z'].T,
        "wbT": I['L0_attn_W_b'].T,
        "waT": I['L0_attn_W_a'].T,
        "convw": I['L0_attn_conv_w'],
        "wmod": I['L0_wmod'],
        "c2d": I['L0_c'][bb].reshape(16, 128).T,
        "advec": np.stack([I['L0_attn_A_log'], I['L0_attn_dt_bias']], 1),
        "negsel": -sel, "selp": sel,
        "maskadd1": np.tile(m1, (1, 32)).astype(ml_dtypes.bfloat16),
        "maskadd2": np.tile(m2, (1, 32)).astype(ml_dtypes.bfloat16),
        "maskadd3": np.tile(m3, (1, 32)).astype(ml_dtypes.bfloat16),
        "onescols": oc,
        "idpair": np.eye(128, dtype=ml_dtypes.bfloat16),
        "gsh": I['L0_ffn_gate_w'][cid * 1024:(cid + 1) * 1024],
        "ush": I['L0_ffn_up_w'][cid * 1024:(cid + 1) * 1024],
        "dsh": I['L0_ffn_down_w'][cid * 256:(cid + 1) * 256],
    }
    return {k: np.ascontiguousarray(v) for k, v in d.items()}


def _l2_inputs(R1, cid):
    bb, kcore = cid // 4, cid % 4
    khTg = np.zeros((4, 128, 128, 64), ml_dtypes.bfloat16)
    qhTg = np.zeros((4, 128, 128, 64), ml_dtypes.bfloat16)
    NTg = np.zeros((4, 128, 128, 128), ml_dtypes.bfloat16)
    ATTg = np.zeros((4, 128, 128, 128), ml_dtypes.bfloat16)
    ktokg = np.zeros((4, 128, 128, 128), ml_dtypes.bfloat16)
    vtokg = np.zeros((4, 128, 128, 128), ml_dtypes.bfloat16)
    colsB = np.zeros((128, 4, 128), np.float32)
    colsE = np.zeros((128, 4, 128), np.float32)
    colsQ = np.zeros((128, 4, 128), np.float32)
    colsBt = np.zeros((128, 4, 128), np.float32)
    afl = np.zeros((1, 1024), np.float32)
    for m in range(4):
        mg = 4 * kcore + m
        for q in range(4):
            r = R1[4 * bb + q]
            gsl = slice(q * 32, (q + 1) * 32)
            kh_t = np.asarray(r["khTc"])[:, mg]
            qh_t = np.asarray(r["qhTc"])[:, mg]
            khTg[m, gsl] = kh_t
            qhTg[m, gsl] = qh_t
            NTg[m, gsl] = np.asarray(r["NT"])[:, :, mg, :]
            ATTg[m, gsl] = np.asarray(r["ATT"])[:, :, mg, :]
            kt = kh_t.transpose(0, 2, 1)
            ktokg[m, gsl] = np.concatenate([kt, kt], 1)
            vf = np.asarray(r["vf"])
            fac = np.asarray(r["FAC"])
            alp = np.asarray(r["ALPHA"])
            for h2 in range(2):
                vh = 2 * mg + h2
                vt = vf[vh * 128:(vh + 1) * 128].T.reshape(32, 64, 128)
                vtokg[m, gsl, h2 * 64:(h2 + 1) * 64, :] = vt
                fc = fac[vh].reshape(4, 32, 64)
                colsB[h2 * 64:(h2 + 1) * 64, m, gsl] = fc[1].T
                colsE[h2 * 64:(h2 + 1) * 64, m, gsl] = fc[2].T
                colsQ[h2 * 64:(h2 + 1) * 64, m, gsl] = fc[3].T
                colsBt[h2 * 64:(h2 + 1) * 64, m, gsl] = fc[0].T
                afl[0, h2 * 512 + m * 128 + q * 32:
                    h2 * 512 + m * 128 + (q + 1) * 32] = alp[vh]
    d = {"khTg": khTg, "qhTg": qhTg, "NTg": NTg, "ATTg": ATTg,
         "ktokg": ktokg, "vtokg": vtokg, "colsB": colsB, "colsE": colsE,
         "colsQ": colsQ, "colsBt": colsBt, "alphaflat": afl}
    return {k: np.ascontiguousarray(v) for k, v in d.items()}


def _l3_inputs(I, R1, R2, cid):
    bb, q = cid // 4, cid % 4
    T0 = q * 2048
    coreT = np.zeros((2048, 4096), ml_dtypes.bfloat16)
    for vh in range(32):
        kcore = vh // 8
        m = (vh // 2) % 4
        h2 = vh % 2
        oc = np.asarray(R2[4 * bb + kcore]["OUTC"])
        arr = oc[m, T0 // 64:T0 // 64 + 32, h2 * 64:(h2 + 1) * 64, :]
        coreT[:, vh * 128:(vh + 1) * 128] = arr.reshape(2048, 128)
    deqg = np.concatenate([np.asarray(R1[c]["DEQG"]) for c in range(8)], 0)
    dequ = np.concatenate([np.asarray(R1[c]["DEQU"]) for c in range(8)], 0)
    deqd = np.concatenate([np.asarray(R1[c]["DEQD"]) for c in range(8)], 0)
    d = {
        "coreT": coreT,
        "ztok": np.asarray(R1[cid]["zf"]).T,
        "xtok": I['L0_attn_x'][bb][T0:T0 + 2048],
        "modv": np.asarray(R1[cid]["MOD"]),
        "normw": np.tile(I['L0_attn_norm_w'], 32)[None, :],
        "woutT": I['L0_attn_W_out'].T,
        "deqgT": deqg.T,
        "dequT": dequ.T,
        "deqdT": deqd.T,
    }
    return {k: np.ascontiguousarray(v) for k, v in d.items()}


def kernel(**inputs):
    I = {k: np.asarray(v) for k, v in inputs.items()}
    cores = list(range(8))
    nc1, _ = build_l1(debug=False)
    r1 = run_bass_kernel_spmd(nc1, [_l1_inputs(I, c) for c in cores],
                              core_ids=cores)
    R1 = r1.results
    nc2 = build_l2()
    r2 = run_bass_kernel_spmd(nc2, [_l2_inputs(R1, c) for c in cores],
                              core_ids=cores)
    R2 = r2.results
    nc3, _ = build_l3(debug=False)
    r3 = run_bass_kernel_spmd(nc3, [_l3_inputs(I, R1, R2, c) for c in cores],
                              core_ids=cores)
    out = np.zeros((2, 8192, 2048), np.float32)
    for c in cores:
        bb, q = c // 4, c % 4
        out[bb, q * 2048:(q + 1) * 2048] = np.asarray(r3.results[c]["OUT"])
    return out
